# version 14
# baseline (speedup 1.0000x reference)
import os
import numpy as np
import ml_dtypes
from contextlib import ExitStack

_PHASE = int(os.environ.get("KPHASE", "99"))
_ESTG = int(os.environ.get("KEXPSTAGE", "9"))

import concourse.bass as bass
import concourse.mybir as mybir
from concourse.bass import IndirectOffsetOnAxis
from concourse.bass_utils import run_bass_kernel_spmd

F32 = mybir.dt.float32
BF16 = mybir.dt.bfloat16
U32 = mybir.dt.uint32
I32 = mybir.dt.int32
AF = mybir.ActivationFunctionType
ALU = mybir.AluOpType
AX = mybir.AxisListType

B, SEQ, H, D = 4, 4096, 1024, 64
T = B * SEQ
E, TOPK, FE = 64, 6, 512
CAP = 384
NSLOT = E * CAP
SF = 2048
NT = T // 128
TPC = T // 8
NTPC = TPC // 128
EXP_SHIFT = -20.0
NCORE = 8

ENGS = ("sync", "scalar", "vector", "tensor", "gpsimd")


class Sched:

    def __init__(self, nc, ctx):
        self.nc = nc
        self.thunks = {e: [] for e in ENGS}
        self.sems = {}
        self.cnt = {}
        self.ctx = ctx

    def sem(self, name):
        self.sems[name] = self.ctx.enter_context(self.nc.semaphore(name))
        self.cnt[name] = 0

    def op(self, eng, emit, incs=(), waits=()):
        wlist = [(s, self.cnt[s] if v is None else v) for s, v in waits]
        ilist = list(incs)
        for s, n in ilist:
            self.cnt[s] += n
        sems = self.sems

        def thunk(e):
            for s, v in wlist:
                if v > 0:
                    e.wait_ge(sems[s], v)
            ins = emit(e)
            for s, n in ilist:
                ins.then_inc(sems[s], n)
        self.thunks[eng].append(thunk)

    def barrier(self):
        target = self.cnt["ph"] + len(ENGS)
        for e in ENGS:
            self.op(e, lambda en: en.nop(), incs=[("ph", 1)])
        for e in ENGS:
            self.op(e, lambda en: en.nop(), waits=[("ph", target)])

    def emit(self, block):
        for e in ENGS:
            ths = self.thunks[e]

            def runner(eng, ths=ths):
                for t_ in ths:
                    t_(eng)
            getattr(block, e)(runner)


def build():
    nc = bass.Bass("TRN2", target_bir_lowering=False, enable_asserts=False,
                   dynamic_dma_scratch_size=32768)

    def dt(name, shape, dtype, kind="Internal"):
        return nc.dram_tensor(name, shape, dtype, kind=kind)

    x_in = dt("x_in", [T, H], F32, "ExternalInput").ap()
    wq_in = dt("wq_in", [H, D], F32, "ExternalInput").ap()
    wk_in = dt("wk_in", [H, D], F32, "ExternalInput").ap()
    wv_in = dt("wv_in", [H, D], F32, "ExternalInput").ap()
    rw_in = dt("rw_in", [E, H], F32, "ExternalInput").ap()
    bias_in = dt("bias_in", [1, E], F32, "ExternalInput").ap()
    w1_in = dt("w1_in", [E * H, FE], F32, "ExternalInput").ap()
    w2_in = dt("w2_in", [E * FE, H], F32, "ExternalInput").ap()
    sw1_in = dt("sw1_in", [SF, H], F32, "ExternalInput").ap()
    sw2_in = dt("sw2_in", [H, SF], F32, "ExternalInput").ap()
    oidx_in = dt("oidx_in", [128, NTPC], U32, "ExternalInput").ap()
    out_ext = dt("out_ext", [TPC, H], F32, "ExternalOutput").ap()

    xln1_bf = dt("xln1_bf", [T, H], BF16).ap()
    xln2_bf = dt("xln2_bf", [T, H], BF16).ap()
    xln2_lo = dt("xln2_lo", [T, H], BF16).ap()
    sw1_bfd = dt("sw1_bfd", [SF, H], BF16).ap()
    sw2_bfd = dt("sw2_bfd", [H, SF], BF16).ap()
    ot_buf = dt("ot_buf", [B * D, SEQ], F32).ap()
    xmid_sp = dt("xmid_sp", [T, H], F32).ap()
    gs_sp = dt("gs_sp", [T, 8], U32).ap()
    xs_buf = dt("xs_buf", [NSLOT, 1056], BF16).ap()
    y_buf = dt("y_buf", [NSLOT + 128, H], BF16).ap()

    with ExitStack() as ctx:
        def sb(name, shape, dtype):
            return ctx.enter_context(nc.sbuf_tensor(name, shape, dtype))

        def ptens(name, shape, dtype):
            return ctx.enter_context(nc.psum_tensor(name, shape, dtype))

        wq_b = sb("wq_b", [128, 8 * D], BF16)
        wk_b = sb("wk_b", [128, 8 * D], BF16)
        wv_b = sb("wv_b", [128, 8 * D], BF16)
        rwt_hi = sb("rwt_hi", [128, 8 * E], BF16)
        rwt_lo = sb("rwt_lo", [128, 8 * E], BF16)
        bias_r = sb("bias_r", [1, E], F32)
        bias_bc = sb("bias_bc", [128, E], F32)
        idt_b = sb("idt_b", [128, 128], BF16)
        idt_f = sb("idt_f", [128, 128], F32)
        tri_b = sb("tri_b", [128, 128], BF16)
        ones_c = sb("ones_c", [128, 1], BF16)
        ones_r1 = sb("ones_r1", [1, 128], F32)
        iota128 = sb("iota128", [128, 128], F32)
        iota128i = sb("iota128i", [128, 128], I32)
        iotap = sb("iotap", [128, 1], F32)
        iotapi = sb("iotapi", [128, 1], I32)
        inval6 = sb("inval6", [128, 6], F32)
        oidx = sb("oidx", [128, NTPC], U32)
        i6_all = sb("i6_all", [128, NT * 6], F32)
        pw_all = sb("pw_all", [128, NT * 6], BF16)
        sl_all = sb("sl_all", [128, NT * 6], U32)
        ln_sq = sb("ln_sq", [128, H], F32)
        ln_s = sb("ln_s", [128, 8], F32)
        mvtW = sb("mvtW", [128, 2048], BF16)
        mvtW2 = sb("mvtW2", [128, 2048], BF16)
        gs_u = sb("gs_u", [128, 8], U32)
        zed = sb("zed", [128, H], BF16)
        gsl = sb("gsl", [128, 8], U32)
        rwt_f = sb("rwt_f", [128, E], F32)
        rwt_l = sb("rwt_l", [128, E], F32)
        eps_c = sb("eps_c", [128, 1], F32)
        shf_c = sb("shf_c", [128, 1], F32)
        ps_a = ptens("ps_a", [128, 2048], F32)
        ps_b = ptens("ps_b", [128, 512], F32)
        ps_c = ptens("ps_c", [128, 512], F32)
        ps_d = ptens("ps_d", [128, 512], F32)
        ps_t = ptens("ps_t", [128, 512], BF16)

        sch = Sched(nc, ctx)
        for s_ in ["ld", "sq", "pq", "pe", "ve", "ac", "po", "ph", "ta", "tb",
                   "lx", "lo", "lg", "ly", "s0", "s1", "w0", "w1"]:
            sch.sem(s_)
        now = lambda s_: sch.cnt[s_]
        hold = {}
        open_scopes = []

        def ve_op(emit, waits=()):
            sch.op("vector", emit, incs=[("ve", 1)], waits=waits)

        def ac_op(emit, waits=()):
            sch.op("scalar", emit, incs=[("ac", 1)], waits=waits)

        def pe_op(emit, waits=()):
            sch.op("tensor", emit, incs=[("pe", 1)], waits=waits)

        def po_op(emit, waits=()):
            sch.op("gpsimd", emit, incs=[("po", 1)], waits=waits)

        def dma(eng, sem, out_ap, in_ap, waits=()):
            sch.op(eng, lambda e: e.dma_start(out=out_ap, in_=in_ap),
                   incs=[(sem, 16)], waits=waits)

        def dmat(out_ap, in_ap, waits=()):
            sch.op("scalar", lambda e: e.dma_start_transpose(out_ap, in_ap),
                   incs=[("sq", 16)], waits=waits)

        s_p0 = ExitStack()
        open_scopes.append(s_p0)
        wst = s_p0.enter_context(nc.sbuf_tensor("wst", [128, 1024], F32))
        wst2 = s_p0.enter_context(nc.sbuf_tensor("wst2", [128, 512], F32))
        rw_st = s_p0.enter_context(nc.sbuf_tensor("rw_st", [64, H], F32))
        dma("sync", "ld", wst[:, 0:512].rearrange("p (c d) -> p c d", c=8),
            wq_in.rearrange("(c p) d -> p c d", p=128))
        dma("sync", "ld",
            wst[:, 512:1024].rearrange("p (c d) -> p c d", c=8),
            wk_in.rearrange("(c p) d -> p c d", p=128))
        dma("sync", "ld", wst2[:, 0:512].rearrange("p (c d) -> p c d", c=8),
            wv_in.rearrange("(c p) d -> p c d", p=128))
        dma("sync", "ld", rw_st[:, :], rw_in)
        dma("sync", "ld", bias_r[:, :], bias_in)
        dma("sync", "ld", oidx[:, :], oidx_in)
        dma("gpsimd", "pq", sw1_bfd, sw1_in)
        dma("gpsimd", "pq", sw2_bfd, sw2_in)
        po_op(lambda e: e.iota(iotapi[:], pattern=[[0, 1]], base=0,
                               channel_multiplier=1))
        po_op(lambda e: e.iota(iota128i[:], pattern=[[1, 128]], base=0,
                               channel_multiplier=0))

        ve_op(lambda e: e.tensor_copy(out=iotap[:], in_=iotapi[:]),
              waits=[("po", None), ("ld", None)])
        ve_op(lambda e: e.tensor_copy(out=iota128[:], in_=iota128i[:]))
        ve_op(lambda e: e.tensor_scalar(idt_f[:], iota128[:], iotap[:, 0:1],
                                        None, op0=ALU.is_equal),
              waits=[("ve", None)])
        ve_op(lambda e: e.tensor_scalar(tri_b[:], iota128[:], iotap[:, 0:1],
                                        None, op0=ALU.is_gt))
        ve_op(lambda e: e.tensor_copy(out=idt_b[:], in_=idt_f[:]),
              waits=[("ve", None)])
        ve_op(lambda e: e.memset(ones_c[:], 1.0))
        ve_op(lambda e: e.memset(ones_r1[:], 1.0))
        ve_op(lambda e: e.memset(inval6[:], 0.0))
        ve_op(lambda e: e.tensor_scalar(inval6[:], inval6[:], iotap[:, 0:1],
                                        100000.0, op0=ALU.add, op1=ALU.add),
              waits=[("ve", None)])
        ve_op(lambda e: e.tensor_copy(out=wq_b[:], in_=wst[:, 0:512]))
        ve_op(lambda e: e.tensor_copy(out=wk_b[:], in_=wst[:, 512:1024]))
        ve_op(lambda e: e.tensor_copy(out=wv_b[:], in_=wst2[:, 0:512]))
        ve_op(lambda e: e.memset(zed[:], 0.0))
        ve_op(lambda e: e.memset(eps_c[:], 1e-5))
        ve_op(lambda e: e.memset(shf_c[:], EXP_SHIFT))
        ve_op(lambda e: e.memset(gs_u[:, 6:8], NSLOT))

        for c in range(8):
            pe_op(lambda e, c=c: e.transpose(
                ps_d[:, 0:64], rw_st[:, 128 * c:128 * (c + 1)],
                idt_f[0:64, 0:64]), waits=[("ve", None)])
            ve_op(lambda e: e.tensor_copy(out=rwt_f[:, :], in_=ps_d[:, 0:E]),
                  waits=[("pe", None)])
            ve_op(lambda e, c=c: e.tensor_copy(
                out=rwt_hi[:, E * c:E * (c + 1)], in_=rwt_f[:, :]),
                waits=[("ve", None)])
            ve_op(lambda e, c=c: e.tensor_tensor(
                rwt_l[:, :], rwt_f[:, :], rwt_hi[:, E * c:E * (c + 1)],
                op=ALU.subtract), waits=[("ve", None)])
            ve_op(lambda e, c=c: e.tensor_copy(
                out=rwt_lo[:, E * c:E * (c + 1)], in_=rwt_l[:, :]),
                waits=[("ve", None)])
        pe_op(lambda e: e.matmul(ps_c[0:128, 0:E], ones_r1[:, :], bias_r[:, :],
                                 start=True, stop=True),
              waits=[("ve", None)])
        ve_op(lambda e: e.tensor_copy(out=bias_bc[:], in_=ps_c[0:128, 0:E]),
              waits=[("pe", None)])

        dma("sync", "ld", y_buf[NSLOT:NSLOT + 128, :], zed[:],
            waits=[("ve", None)])
        zw = now("ve")
        for rr in range(192):
            sch.op("scalar", lambda e, rr=rr: e.dma_start(
                out=xs_buf[128 * rr:128 * (rr + 1), 0:1024], in_=zed[:, :]),
                incs=[("sq", 16)], waits=[("ve", zw)] if rr == 0 else ())
            sch.op("scalar", lambda e, rr=rr: e.dma_start(
                out=xs_buf[128 * rr:128 * (rr + 1), 1024:1056],
                in_=zed[:, 0:32]), incs=[("sq", 16)])
        hold["xsfill"] = now("sq")
        sch.barrier()
        s_p0.close()
        open_scopes.remove(s_p0)


        def _truncate():
            for s in reversed(open_scopes):
                s.close()
            for gown in range(NTPC):
                dma("sync", "ld", out_ext[128 * gown:128 * (gown + 1), :],
                    ln_sq[:, :], waits=[("ve", None)])
            sch.op("sync", lambda e: e.nop(),
                   waits=[("ld", None), ("pq", None), ("sq", None),
                          ("lx", None), ("lo", None), ("lg", None),
                          ("ly", None), ("s0", None), ("s1", None),
                          ("w0", None), ("w1", None)])
            block = ctx.enter_context(nc.Block())
            sch.emit(block)

        if _PHASE <= 0:
            _truncate()
            return nc

        def emit_ln(src_ap, dst_f32, dst_b16=None, pre_waits=()):
            ve_op(lambda e: e.reduce_sum(ln_s[:, 0:1], src_ap, axis=AX.X),
                  waits=list(pre_waits) + [("ac", None)])
            ac_op(lambda e: e.activation(out=ln_sq[:, :], in_=src_ap,
                                         func=AF.Square,
                                         accum_out=ln_s[:, 1:2]),
                  waits=list(pre_waits) + [("ve", None)])
            ve_op(lambda e: e.tensor_scalar(ln_s[:, 2:3], ln_s[:, 0:1],
                                            1.0 / H, None, op0=ALU.mult),
                  waits=[("ve", None)])
            ve_op(lambda e: e.tensor_scalar(ln_s[:, 3:4], ln_s[:, 2:3],
                                            ln_s[:, 2:3], None, op0=ALU.mult),
                  waits=[("ve", None)])
            ve_op(lambda e: e.tensor_scalar(ln_s[:, 4:5], ln_s[:, 1:2],
                                            1.0 / H, None, op0=ALU.mult),
                  waits=[("ac", None)])
            ve_op(lambda e: e.tensor_tensor(ln_s[:, 5:6], ln_s[:, 4:5],
                                            ln_s[:, 3:4], op=ALU.subtract),
                  waits=[("ve", None)])
            ac_op(lambda e: e.activation(out=ln_s[:, 6:7], in_=ln_s[:, 5:6],
                                         func=AF.Sqrt, bias=eps_c[:, 0:1], scale=1.0),
                  waits=[("ve", None)])
            ve_op(lambda e: e.reciprocal(ln_s[:, 7:8], ln_s[:, 6:7]),
                  waits=[("ac", None)])
            ve_op(lambda e: e.tensor_scalar(dst_f32, src_ap, ln_s[:, 2:3],
                                            ln_s[:, 7:8], op0=ALU.subtract,
                                            op1=ALU.mult),
                  waits=[("ve", None)])
            if dst_b16 is not None:
                ve_op(lambda e: e.tensor_copy(out=dst_b16, in_=dst_f32),
                      waits=[("ve", None)])

        s_ln = ExitStack()
        open_scopes.append(s_ln)
        xstg = s_ln.enter_context(nc.sbuf_tensor("xstg", [128, 4 * H], F32))
        ostg = s_ln.enter_context(nc.sbuf_tensor("ostg", [128, 4 * H], F32))
        hstg = s_ln.enter_context(nc.sbuf_tensor("hstg", [128, 4 * H], BF16))
        lstg = s_ln.enter_context(nc.sbuf_tensor("lstg", [128, 4 * H], BF16))
        otok = s_ln.enter_context(nc.sbuf_tensor("otok", [128, D], F32))
        ost64 = s_ln.enter_context(nc.sbuf_tensor("ost64", [64, 128], F32))
        NG = NT // 4
        for g in range(NG):
            for j in range(4):
                tile = 4 * g + j
                dma("sync", "lx", xstg[:, H * j:H * (j + 1)],
                    x_in[128 * tile:128 * (tile + 1), :],
                    waits=[("ta", g)] if (j == 0 and g >= 1) else ())
            ldw = now("lx")
            for j in range(4):
                emit_ln(xstg[:, H * j:H * (j + 1)],
                        ostg[:, H * j:H * (j + 1)],
                        hstg[:, H * j:H * (j + 1)],
                        pre_waits=[("lx", ldw)] if j == 0 else ())
            dma("sync", "lo",
                xln1_bf[512 * g:512 * (g + 1), :].rearrange(
                    "(j p) h -> p j h", p=128),
                hstg[:, :].rearrange("p (j h) -> p j h", j=4),
                waits=[("ve", None), ("ac", None)])
            sch.op("vector", lambda e: e.nop(), incs=[("ta", 1)],
                   waits=[("lo", now("lo")), ("ac", None)])
        sch.barrier()

        if _PHASE <= 1:
            _truncate()
            return nc

        s_att = ExitStack()
        open_scopes.append(s_att)
        qt = s_att.enter_context(nc.sbuf_tensor("qt", [64, SEQ], BF16))
        kt = s_att.enter_context(nc.sbuf_tensor("kt", [64, SEQ], BF16))
        vt = s_att.enter_context(nc.sbuf_tensor("vt", [64, SEQ], BF16))
        tspan = s_att.enter_context(
            nc.sbuf_tensor("tspan", [128, 8 * 2048], BF16))
        at_t0 = s_att.enter_context(nc.sbuf_tensor("at_t0", [128, SEQ], BF16))
        at_t1 = s_att.enter_context(nc.sbuf_tensor("at_t1", [128, SEQ], BF16))
        rs8 = s_att.enter_context(nc.sbuf_tensor("rs8", [128, 8], F32))
        rsum = s_att.enter_context(nc.sbuf_tensor("rsum", [128, 1], F32))
        vs_t0 = s_att.enter_context(nc.sbuf_tensor("vs_t0", [128, D], BF16))
        vs_t1 = s_att.enter_context(nc.sbuf_tensor("vs_t1", [128, D], BF16))
        osb = s_att.enter_context(nc.sbuf_tensor("osb", [128, SEQ], F32))
        osb2 = s_att.enter_context(nc.sbuf_tensor("osb2", [64, SEQ], F32))
        for b in range(B):
            for th in range(2):
                rows = b * SEQ + th * 2048
                for hc in range(8):
                    dmat(tspan[:, 2048 * hc:2048 * (hc + 1)],
                         xln1_bf[rows:rows + 2048, 128 * hc:128 * (hc + 1)],
                         waits=[("pe", None), ("lo", None)] if hc == 0
                         else ())
                sqw = now("sq")
                for hc in range(8):
                    for nn in range(4):
                        w = [("sq", sqw), ("ve", None)] \
                            if (hc == 0 and nn == 0) else ()
                        sl = slice(2048 * hc + 512 * nn,
                                   2048 * hc + 512 * (nn + 1))
                        pe_op(lambda e, hc=hc, nn=nn, sl=sl: e.matmul(
                            ps_a[0:64, 512 * nn:512 * (nn + 1)],
                            wq_b[:, 64 * hc:64 * (hc + 1)], tspan[:, sl],
                            start=(hc == 0), stop=(hc == 7)), waits=w)
                        pe_op(lambda e, hc=hc, nn=nn, sl=sl: e.matmul(
                            ps_a[64:128, 512 * nn:512 * (nn + 1)],
                            wk_b[:, 64 * hc:64 * (hc + 1)], tspan[:, sl],
                            start=(hc == 0), stop=(hc == 7),
                            skip_group_check=True))
                ve_op(lambda e, th=th: e.tensor_copy(
                    out=qt[:, 2048 * th:2048 * (th + 1)], in_=ps_a[0:64, :]),
                    waits=[("pe", None)])
                ve_op(lambda e, th=th: e.tensor_copy(
                    out=kt[:, 2048 * th:2048 * (th + 1)], in_=ps_a[64:128, :]))
                vew = now("ve")
                for hc in range(8):
                    for nn in range(4):
                        w = [("ve", vew)] if (hc == 0 and nn == 0) else ()
                        sl = slice(2048 * hc + 512 * nn,
                                   2048 * hc + 512 * (nn + 1))
                        pe_op(lambda e, hc=hc, nn=nn, sl=sl: e.matmul(
                            ps_a[0:64, 512 * nn:512 * (nn + 1)],
                            wv_b[:, 64 * hc:64 * (hc + 1)], tspan[:, sl],
                            start=(hc == 0), stop=(hc == 7)), waits=w)
                ve_op(lambda e, th=th: e.tensor_copy(
                    out=vt[:, 2048 * th:2048 * (th + 1)], in_=ps_a[0:64, :]),
                    waits=[("pe", None)])
            ve_op(lambda e: e.memset(osb[:, :], 0.0),
                  waits=[("ve", None), ("ld", None)])
            ac_after_exp = [0] * 8
            for kp in range(16):
                for kk in range(2):
                    kt_i = 2 * kp + kk
                    at_t = at_t0 if kk == 0 else at_t1
                    vs_t = vs_t0 if kk == 0 else vs_t1
                    ve_after_rsum = now("ve")
                    for cc in range(8):
                        psS = ps_b if (cc % 2 == 0) else ps_c
                        w = []
                        if cc >= 2:
                            w.append(("ac", ac_after_exp[cc - 2]))
                        if cc == 0:
                            w.append(("ve", None))
                        pe_op(lambda e, kt_i=kt_i, cc=cc, psS=psS: e.matmul(
                            psS[:, :], kt[:, 128 * kt_i:128 * (kt_i + 1)],
                            qt[:, 512 * cc:512 * (cc + 1)],
                            start=True, stop=True), waits=w)
                        ac_op(lambda e, cc=cc, psS=psS, at_t=at_t:
                              e.activation(
                                  out=at_t[:, 512 * cc:512 * (cc + 1)],
                                  in_=psS[:, :], func=AF.Exp,
                                  bias=shf_c[:, 0:1], scale=1.0,
                                  accum_out=rs8[:, cc:cc + 1]),
                              waits=[("pe", None), ("ve", ve_after_rsum)])
                        ac_after_exp[cc] = now("ac")
                    ve_op(lambda e: e.reduce_sum(rsum[:, 0:1], rs8[:, :],
                                                 axis=AX.X),
                          waits=[("ac", None)])
                    ve_op(lambda e: e.reciprocal(rsum[:, 0:1], rsum[:, 0:1]),
                          waits=[("ve", None)])
                    pe_op(lambda e, kt_i=kt_i: e.transpose(
                        ps_t[:, 0:64], vt[:, 128 * kt_i:128 * (kt_i + 1)],
                        idt_b[0:64, 0:64]), waits=[("ve", None)])
                    ve_op(lambda e, vs_t=vs_t: e.tensor_scalar(
                        vs_t[:, :], ps_t[:, 0:64], rsum[:, 0:1], None,
                        op0=ALU.mult), waits=[("pe", None), ("ve", None)])
                vw = now("ve")
                for half in range(2):
                    for kk in range(2):
                        at_t = at_t0 if kk == 0 else at_t1
                        vs_t = vs_t0 if kk == 0 else vs_t1
                        for c2 in range(4):
                            w = []
                            if half == 0 and kk == 0 and c2 == 0:
                                w = [("ve", vw)]
                            if half == 1 and kk == 0 and c2 == 0:
                                w = [("ve", None)]
                            pe_op(lambda e, kk=kk, half=half, c2=c2,
                                  at_t=at_t, vs_t=vs_t: e.matmul(
                                      ps_a[64 * kk:64 * kk + 64,
                                           512 * c2:512 * (c2 + 1)],
                                      vs_t[:, :],
                                      at_t[:, 2048 * half + 512 * c2:
                                           2048 * half + 512 * (c2 + 1)],
                                      start=True, stop=True,
                                      skip_group_check=(kk == 1)), waits=w)
                    ve_op(lambda e, half=half: e.tensor_tensor(
                        osb[:, 2048 * half:2048 * (half + 1)],
                        osb[:, 2048 * half:2048 * (half + 1)],
                        ps_a[:, 0:2048], op=ALU.add),
                        waits=[("pe", None), ("ve", None)])
            dma("sync", "ld", osb2[:, :], osb[64:128, :],
                waits=[("ve", None)])
            ve_op(lambda e: e.tensor_tensor(osb[0:64, :], osb[0:64, :],
                                            osb2[:, :], op=ALU.add),
                  waits=[("ld", None)])
            dma("sync", "ld", ot_buf[D * b:D * (b + 1), :], osb[0:64, :],
                waits=[("ve", None)])
        sch.barrier()
        s_att.close()
        open_scopes.remove(s_att)

        if _PHASE <= 2:
            _truncate()
            return nc

        for g in range(NG):
            ldg0 = now("ld")
            for j in range(4):
                tile = 4 * g + j
                b = tile // 32
                col = (tile % 32) * 128
                dma("sync", "lx", xstg[:, H * j:H * (j + 1)],
                    x_in[128 * tile:128 * (tile + 1), :],
                    waits=([("tb", g)] if (j == 0 and g >= 1) else [])
                    + [("pe", None)])
                dma("sync", "lx", ost64[:, :],
                    ot_buf[D * b:D * (b + 1), col:col + 128],
                    waits=[("pe", None), ("ld", None)])
                ldw = now("lx")
                pe_op(lambda e: e.transpose(ps_d[:, 0:64], ost64[:, :],
                                            idt_f[0:64, 0:64]),
                      waits=[("lx", ldw), ("ve", None)])
                ve_op(lambda e: e.tensor_copy(out=otok[:, :],
                                              in_=ps_d[:, 0:64]),
                      waits=[("pe", None)])
                for r in range(16):
                    ve_op(lambda e, j=j, r=r: e.tensor_tensor(
                        xstg[:, H * j + 64 * r:H * j + 64 * (r + 1)],
                        xstg[:, H * j + 64 * r:H * j + 64 * (r + 1)],
                        otok[:, :], op=ALU.add),
                        waits=[("ve", None)] if r == 0 else ())
                emit_ln(xstg[:, H * j:H * (j + 1)],
                        ostg[:, H * j:H * (j + 1)],
                        hstg[:, H * j:H * (j + 1)],
                        pre_waits=[("ve", None)])
                ve_op(lambda e, j=j: e.tensor_tensor(
                    ostg[:, H * j:H * (j + 1)], ostg[:, H * j:H * (j + 1)],
                    hstg[:, H * j:H * (j + 1)], op=ALU.subtract),
                    waits=[("ve", None)])
                ve_op(lambda e, j=j: e.tensor_copy(
                    out=lstg[:, H * j:H * (j + 1)],
                    in_=ostg[:, H * j:H * (j + 1)]),
                    waits=[("ve", None)])
            vew = now("ve")
            dma("sync", "lo",
                xmid_sp[512 * g:512 * (g + 1), :].rearrange(
                    "(j p) h -> p j h", p=128),
                xstg[:, :].rearrange("p (j h) -> p j h", j=4),
                waits=[("ve", vew), ("ac", None)])
            dma("sync", "lo",
                xln2_bf[512 * g:512 * (g + 1), :].rearrange(
                    "(j p) h -> p j h", p=128),
                hstg[:, :].rearrange("p (j h) -> p j h", j=4))
            dma("sync", "lo",
                xln2_lo[512 * g:512 * (g + 1), :].rearrange(
                    "(j p) h -> p j h", p=128),
                lstg[:, :].rearrange("p (j h) -> p j h", j=4))
            sch.op("vector", lambda e: e.nop(), incs=[("tb", 1)],
                   waits=[("lo", now("lo"))])
        sch.barrier()
        if _PHASE <= 3:
            _truncate()
            return nc
        s_ln.close()
        open_scopes.remove(s_ln)
        s_p3 = ExitStack()
        open_scopes.append(s_p3)
        mm_all = s_p3.enter_context(
            nc.sbuf_tensor("mm_all", [128, NT * E], BF16))
        lgT = s_p3.enter_context(nc.sbuf_tensor("lgT", [64, 2048], F32))
        sc_t = s_p3.enter_context(nc.sbuf_tensor("sc_t", [128, E], F32))
        scb_t = s_p3.enter_context(nc.sbuf_tensor("scb_t", [128, E], F32))
        m8 = s_p3.enter_context(nc.sbuf_tensor("m8", [128, 8], F32))
        i8 = s_p3.enter_context(nc.sbuf_tensor("i8", [128, 8], U32))
        i8f = s_p3.enter_context(nc.sbuf_tensor("i8f", [128, 8], F32))
        cmp6 = s_p3.enter_context(nc.sbuf_tensor("cmp6", [128, 384], F32))
        prod6 = s_p3.enter_context(nc.sbuf_tensor("prod6", [128, 384], F32))
        ext6 = s_p3.enter_context(nc.sbuf_tensor("ext6", [128, 6], F32))
        pos6 = s_p3.enter_context(nc.sbuf_tensor("pos6", [128, 6], F32))
        pw6 = s_p3.enter_context(nc.sbuf_tensor("pw6", [128, 6], F32))
        slot6 = s_p3.enter_context(nc.sbuf_tensor("slot6", [128, 6], F32))
        vm6 = s_p3.enter_context(nc.sbuf_tensor("vm6", [128, 6], F32))
        nv6 = s_p3.enter_context(nc.sbuf_tensor("nv6", [128, 6], F32))
        tt1 = s_p3.enter_context(nc.sbuf_tensor("tt1", [128, 6], F32))
        tt2 = s_p3.enter_context(nc.sbuf_tensor("tt2", [128, 6], F32))
        sl_f = s_p3.enter_context(nc.sbuf_tensor("sl_f", [128, 6], F32))
        gs_f = s_p3.enter_context(nc.sbuf_tensor("gs_f", [128, 6], F32))
        sum6 = s_p3.enter_context(nc.sbuf_tensor("sum6", [128, 1], F32))
        mtree = s_p3.enter_context(nc.sbuf_tensor("mtree", [128, 192], F32))
        mfin = s_p3.enter_context(nc.sbuf_tensor("mfin", [128, 64], F32))
        pos_t = s_p3.enter_context(nc.sbuf_tensor("pos_t", [128, E], F32))
        cntT = s_p3.enter_context(nc.sbuf_tensor("cntT", [64, NT], F32))
        cnt_sc = s_p3.enter_context(nc.sbuf_tensor("cnt_sc", [64, NT], F32))
        cnt_ex = s_p3.enter_context(nc.sbuf_tensor("cnt_ex", [64, NT], F32))
        base_rows = s_p3.enter_context(
            nc.sbuf_tensor("base_rows", [128, E], F32))
        base_st = s_p3.enter_context(nc.sbuf_tensor("base_st", [1, E], F32))

        for tb in range(8):
            rows = 2048 * tb
            pewM = now("pe")
            for hc in range(8):
                buf = mvtW if hc % 2 == 0 else mvtW2
                psem = "s0" if hc % 2 == 0 else "s1"
                sch.op("scalar", lambda e, buf=buf, rows=rows, hc=hc:
                       e.dma_start_transpose(
                           buf[:, :],
                           xln2_bf[rows:rows + 2048,
                                   128 * hc:128 * (hc + 1)]),
                       incs=[(psem, 16)],
                       waits=[("pe", hold.get(("lgmm", tb, hc - 2), 0)),
                              ("ve", None), ("lo", None)] if hc < 2 else
                             [("pe", hold.get(("lgmm", tb, hc - 2), 0))])
                sqw = now(psem)
                for cc in range(4):
                    w = [(psem, sqw)] if cc == 0 else ()
                    pe_op(lambda e, hc=hc, cc=cc, buf=buf: e.matmul(
                        ps_a[0:64, 512 * cc:512 * (cc + 1)],
                        rwt_hi[:, E * hc:E * (hc + 1)],
                        buf[:, 512 * cc:512 * (cc + 1)],
                        start=(hc == 0), stop=False), waits=w)
                    pe_op(lambda e, hc=hc, cc=cc, buf=buf: e.matmul(
                        ps_a[0:64, 512 * cc:512 * (cc + 1)],
                        rwt_lo[:, E * hc:E * (hc + 1)],
                        buf[:, 512 * cc:512 * (cc + 1)],
                        start=False, stop=False))
                hold[("lgmm", tb, hc)] = now("pe")
            for hc in range(8):
                buf = mvtW if hc % 2 == 0 else mvtW2
                psem = "s0" if hc % 2 == 0 else "s1"
                sch.op("scalar", lambda e, buf=buf, rows=rows, hc=hc:
                       e.dma_start_transpose(
                           buf[:, :],
                           xln2_lo[rows:rows + 2048,
                                   128 * hc:128 * (hc + 1)]),
                       incs=[(psem, 16)],
                       waits=[("pe", hold.get(("lgml", tb, hc - 2),
                                              hold[("lgmm", tb, 7)]))])
                sqw = now(psem)
                for cc in range(4):
                    w = [(psem, sqw)] if cc == 0 else ()
                    pe_op(lambda e, hc=hc, cc=cc, buf=buf: e.matmul(
                        ps_a[0:64, 512 * cc:512 * (cc + 1)],
                        rwt_hi[:, E * hc:E * (hc + 1)],
                        buf[:, 512 * cc:512 * (cc + 1)],
                        start=False, stop=(hc == 7)), waits=w)
                hold[("lgml", tb, hc)] = now("pe")
            ve_op(lambda e: e.tensor_copy(out=lgT[:, :], in_=ps_a[0:64, :]),
                  waits=[("pe", None)])
            for j16 in range(16):
                g = 16 * tb + j16
                pe_op(lambda e, j16=j16: e.transpose(
                    ps_d[:, 0:64], lgT[:, 128 * j16:128 * (j16 + 1)],
                    idt_f[0:64, 0:64]),
                    waits=[("ve", None), ("ac", None)])
                ac_op(lambda e: e.activation(out=sc_t[:, :],
                                             in_=ps_d[:, 0:64],
                                             func=AF.Sigmoid),
                      waits=[("pe", None)])
                ve_op(lambda e: e.tensor_tensor(scb_t[:, :], sc_t[:, :],
                                                bias_bc[:, :], op=ALU.add),
                      waits=[("ac", None), ("ve", None)])
                ve_op(lambda e: e.max(out=m8[:, :], in_=scb_t[:, :]),
                      waits=[("ve", None)])
                ve_op(lambda e: e.max_index(out=i8[:, :], in_max=m8[:, :],
                                            in_values=scb_t[:, :]),
                      waits=[("ve", None)])
                ve_op(lambda e: e.tensor_copy(out=i8f[:, :], in_=i8[:, :]),
                      waits=[("ve", None)])
                ve_op(lambda e, g=g: e.tensor_copy(
                    out=i6_all[:, 6 * g:6 * g + 6], in_=i8f[:, 0:6]),
                    waits=[("ve", None)])
                for k in range(TOPK):
                    ve_op(lambda e, k=k: e.tensor_scalar(
                        cmp6[:, 64 * k:64 * (k + 1)], iota128[:, 0:64],
                        i8f[:, k:k + 1], None, op0=ALU.is_equal),
                        waits=[("ve", None)] if k == 0 else ())
                for k in range(TOPK):
                    ve_op(lambda e, k=k: e.tensor_tensor(
                        prod6[:, 64 * k:64 * (k + 1)],
                        cmp6[:, 64 * k:64 * (k + 1)], sc_t[:, :],
                        op=ALU.mult),
                        waits=[("ve", None)] if k == 0 else ())
                for k in range(TOPK):
                    ve_op(lambda e, k=k: e.reduce_sum(
                        ext6[:, k:k + 1], prod6[:, 64 * k:64 * (k + 1)],
                        axis=AX.X), waits=[("ve", None)] if k == 0 else ())
                ve_op(lambda e: e.tensor_tensor(
                    mtree[:, 0:192], cmp6[:, 0:192], cmp6[:, 192:384],
                    op=ALU.add), waits=[("ve", None)])
                ve_op(lambda e: e.tensor_tensor(
                    mfin[:, :], mtree[:, 0:64], mtree[:, 64:128],
                    op=ALU.add), waits=[("ve", None)])
                ve_op(lambda e: e.tensor_tensor(
                    mfin[:, :], mfin[:, :], mtree[:, 128:192],
                    op=ALU.add), waits=[("ve", None)])
                ve_op(lambda e, g=g: e.tensor_copy(
                    out=mm_all[:, E * g:E * (g + 1)], in_=mfin[:, :]),
                    waits=[("ve", None)])
                ve_op(lambda e: e.reduce_sum(sum6[:, 0:1], ext6[:, :],
                                             axis=AX.X),
                      waits=[("ve", None)])
                ve_op(lambda e: e.tensor_scalar(sum6[:, 0:1], sum6[:, 0:1],
                                                1e-20, None, op0=ALU.add),
                      waits=[("ve", None)])
                ve_op(lambda e: e.reciprocal(sum6[:, 0:1], sum6[:, 0:1]),
                      waits=[("ve", None)])
                ve_op(lambda e: e.tensor_scalar(pw6[:, :], ext6[:, :],
                                                sum6[:, 0:1], None,
                                                op0=ALU.mult),
                      waits=[("ve", None)])
                ve_op(lambda e, g=g: e.tensor_copy(
                    out=pw_all[:, 6 * g:6 * g + 6], in_=pw6[:, :]),
                    waits=[("ve", None)])
                pe_op(lambda e, g=g: e.matmul(
                    ps_c[0:64, 0:1], mm_all[:, E * g:E * (g + 1)],
                    ones_c[:, :], start=True, stop=True),
                    waits=[("ve", None)])
                ve_op(lambda e, g=g: e.tensor_copy(
                    out=cntT[:, g:g + 1], in_=ps_c[0:64, 0:1]),
                    waits=[("pe", None)])
        sch.barrier()

        if _PHASE <= 4:
            _truncate()
            return nc

        ve_op(lambda e: e.tensor_tensor_scan(
            cnt_sc[:, :], cntT[:, :], cntT[:, :], 0.0,
            op0=ALU.add, op1=ALU.bypass), waits=[("ve", None)])
        ve_op(lambda e: e.tensor_tensor(cnt_ex[:, :], cnt_sc[:, :],
                                        cntT[:, :], op=ALU.subtract),
              waits=[("ve", None)])
        pe_op(lambda e: e.transpose(ps_d[:, 0:64], cnt_ex[:, :],
                                     idt_f[0:64, 0:64]),
              waits=[("ve", None)])
        ve_op(lambda e: e.tensor_copy(out=base_rows[:, :],
                                      in_=ps_d[:, 0:64]),
              waits=[("pe", None)])
        for g in range(NT):
            dma("sync", "ld", base_st[:, :], base_rows[g:g + 1, :],
                waits=[("pe", None), ("ve", None)])
            ldw = now("ld")
            pe_op(lambda e, g=g: e.matmul(
                ps_c[0:128, 0:E], tri_b[:, :], mm_all[:, E * g:E * (g + 1)],
                start=True, stop=False), waits=[("ve", None)])
            pe_op(lambda e: e.matmul(ps_c[0:128, 0:E], ones_r1[:, :],
                                     base_st[:, :], start=False, stop=True),
                  waits=[("ld", ldw)])
            ve_op(lambda e: e.tensor_copy(out=pos_t[:, :],
                                          in_=ps_c[0:128, 0:E]),
                  waits=[("pe", None), ("ve", None)])
            for k in range(TOPK):
                ve_op(lambda e, k=k, g=g: e.tensor_scalar(
                    cmp6[:, 64 * k:64 * (k + 1)], iota128[:, 0:64],
                    i6_all[:, 6 * g + k:6 * g + k + 1], None,
                    op0=ALU.is_equal),
                    waits=[("ve", None)] if k == 0 else ())
            for k in range(TOPK):
                ve_op(lambda e, k=k: e.tensor_tensor(
                    prod6[:, 64 * k:64 * (k + 1)],
                    cmp6[:, 64 * k:64 * (k + 1)], pos_t[:, :], op=ALU.mult),
                    waits=[("ve", None)] if k == 0 else ())
            for k in range(TOPK):
                ve_op(lambda e, k=k: e.reduce_sum(
                    pos6[:, k:k + 1], prod6[:, 64 * k:64 * (k + 1)],
                    axis=AX.X), waits=[("ve", None)] if k == 0 else ())
            ve_op(lambda e, g=g: e.tensor_scalar(
                slot6[:, :], i6_all[:, 6 * g:6 * g + 6], float(CAP), None,
                op0=ALU.mult), waits=[("ve", None)])
            ve_op(lambda e: e.tensor_tensor(slot6[:, :], slot6[:, :],
                                            pos6[:, :], op=ALU.add),
                  waits=[("ve", None)])
            ve_op(lambda e: e.tensor_scalar(vm6[:, :], pos6[:, :],
                                            float(CAP), None, op0=ALU.is_lt),
                  waits=[("ve", None)])
            ve_op(lambda e: e.tensor_scalar(nv6[:, :], vm6[:, :], -1.0, 1.0,
                                            op0=ALU.mult, op1=ALU.add),
                  waits=[("ve", None)])
            ve_op(lambda e: e.tensor_tensor(tt1[:, :], vm6[:, :],
                                            slot6[:, :], op=ALU.mult),
                  waits=[("ve", None)])
            ve_op(lambda e: e.tensor_tensor(tt2[:, :], nv6[:, :],
                                            inval6[:, :], op=ALU.mult),
                  waits=[("ve", None)])
            ve_op(lambda e: e.tensor_tensor(sl_f[:, :], tt1[:, :],
                                            tt2[:, :], op=ALU.add),
                  waits=[("ve", None)])
            ve_op(lambda e: e.tensor_scalar(tt2[:, :], nv6[:, :],
                                            float(NSLOT), None,
                                            op0=ALU.mult),
                  waits=[("ve", None)])
            ve_op(lambda e: e.tensor_tensor(gs_f[:, :], tt1[:, :],
                                            tt2[:, :], op=ALU.add),
                  waits=[("ve", None)])
            ve_op(lambda e, g=g: e.tensor_copy(
                out=sl_all[:, 6 * g:6 * g + 6], in_=sl_f[:, :]),
                waits=[("ve", None)])
            ve_op(lambda e: e.tensor_copy(out=gs_u[:, 0:6], in_=gs_f[:, :]),
                  waits=[("ve", None), ("lg", None)])
            dma("sync", "lg", gs_sp[128 * g:128 * (g + 1), :], gs_u[:, :],
                waits=[("ve", None)])
        sch.barrier()

        if _PHASE <= 5:
            _truncate()
            return nc

        s_p3.close()
        open_scopes.remove(s_p3)
        s_p4 = ExitStack()
        open_scopes.append(s_p4)
        src_t = s_p4.enter_context(nc.sbuf_tensor("src_t", [128, 1056], BF16))
        ve_op(lambda e: e.memset(src_t[:, 1024:1056], 0.0))

        def mk_breg(e):
            if "breg" not in hold:
                hold["breg"] = e.to_reg(NSLOT - 1)
            return hold["breg"]

        for g in range(NT):
            dma("sync", "ld", src_t[:, 0:1024],
                xln2_bf[128 * g:128 * (g + 1), :],
                waits=[("pq", None), ("lo", None)])
            ldw = now("ld")
            for k in range(TOPK):
                ve_op(lambda e, g=g, k=k: e.tensor_copy(
                    out=src_t[:, 1024:1025],
                    in_=pw_all[:, 6 * g + k:6 * g + k + 1]),
                    waits=[("ld", ldw), ("pq", None), ("ve", None)])

                def scat(e, g=g, k=k):
                    breg = mk_breg(e)
                    return e.indirect_dma_start(
                        out=xs_buf,
                        out_offset=IndirectOffsetOnAxis(
                            ap=sl_all[:, 6 * g + k:6 * g + k + 1], axis=0),
                        in_=src_t[:, 0:1056], in_offset=None,
                        bounds_check=breg, oob_is_err=False)
                sch.op("gpsimd", scat, incs=[("pq", 16)],
                       waits=[("ve", None)] +
                       ([("sq", None)] if (g == 0 and k == 0)
                        else []))
        sch.barrier()

        if _PHASE <= 6:
            _truncate()
            return nc

        s_p4.close()
        open_scopes.remove(s_p4)
        s_p5 = ExitStack()
        open_scopes.append(s_p5)
        wex1 = s_p5.enter_context(
            nc.sbuf_tensor("wex1", [128, 8 * FE], BF16))
        wex2 = s_p5.enter_context(nc.sbuf_tensor("wex2", [128, 4 * H], BF16))
        wex1b = s_p5.enter_context(
            nc.sbuf_tensor("wex1b", [128, 8 * FE], BF16))
        wex2b = s_p5.enter_context(
            nc.sbuf_tensor("wex2b", [128, 4 * H], BF16))
        xs_sb = s_p5.enter_context(
            nc.sbuf_tensor("xs_sb", [128, 3 * 1056], BF16))
        xst = s_p5.enter_context(nc.sbuf_tensor("xst", [128, 8 * CAP], BF16))
        pw_f = s_p5.enter_context(nc.sbuf_tensor("pw_f", [128, 4], F32))
        h1g = s_p5.enter_context(nc.sbuf_tensor("h1g", [128, 4 * CAP], BF16))
        y_sb = s_p5.enter_context(nc.sbuf_tensor("y_sb", [128, H], BF16))
        sch.op("sync", lambda e: e.nop(), waits=[("pq", None)])
        buf_last_use = {0: 0, 1: 0}
        for e_i in range(E):
            wb1 = wex1 if e_i % 2 == 0 else wex1b
            wb2 = wex2 if e_i % 2 == 0 else wex2b
            wsem = "w0" if e_i % 2 == 0 else "w1"
            for c in range(8):
                sch.op("gpsimd", lambda e, wb1=wb1, e_i=e_i, c=c: e.dma_start(
                    out=wb1[:, 512 * c:512 * (c + 1)],
                    in_=w1_in[H * e_i + 128 * c:H * e_i + 128 * (c + 1), :]),
                    incs=[(wsem, 16)],
                    waits=([("pe", buf_last_use[e_i % 2])] +
                           ([("pq", None)] if e_i == 0 else []))
                    if c == 0 else ())
            for c in range(4):
                sch.op("gpsimd", lambda e, wb2=wb2, e_i=e_i, c=c: e.dma_start(
                    out=wb2[:, 1024 * c:1024 * (c + 1)],
                    in_=w2_in[FE * e_i + 128 * c:FE * e_i + 128 * (c + 1), :]),
                    incs=[(wsem, 16)])
            pqw = now(wsem)
            ldx0 = now("ld")
            for r in range(3):
                dma("sync", "ld", xs_sb[:, 1056 * r:1056 * (r + 1)],
                    xs_buf[CAP * e_i + 128 * r:CAP * e_i + 128 * (r + 1), :],
                    waits=[("pe", hold.get("xs_used", 0))] if r == 0 else ())
            ldw = now("ld")
            if _ESTG <= 1:
                continue
            for r in range(3):
                ve_op(lambda e, r=r: e.tensor_copy(
                    out=pw_f[:, r:r + 1],
                    in_=xs_sb[:, 1056 * r + 1024:1056 * r + 1025]),
                    waits=[("ld", ldw), ("ve", None)] if r == 0
                    else [("ve", None)])
            if _ESTG <= 2:
                continue
            psdb = ps_d[:, 0:256].bitcast(BF16)
            tcnt = 0
            for hc in range(8):
                for r in range(3):
                    dstv = (ps_t[:, 0:128] if tcnt % 2 == 0
                            else psdb[:, 0:128])
                    pe_op(lambda e, hc=hc, r=r, dstv=dstv: e.transpose(
                        dstv,
                        xs_sb[:, 1056 * r + 128 * hc:1056 * r + 128 * (hc + 1)],
                        idt_b[:, :]),
                        waits=[("ld", ldw), ("ve", None)] if tcnt == 0
                        else ([("ve", hold.get(("xev", e_i, tcnt - 2), 0))]
                              if tcnt >= 2 else ()))
                    ve_op(lambda e, hc=hc, r=r, dstv=dstv: e.tensor_copy(
                        out=xst[:, CAP * hc + 128 * r:CAP * hc + 128 * (r + 1)],
                        in_=dstv),
                        waits=[("pe", None)])
                    hold[("xev", e_i, tcnt)] = now("ve")
                    tcnt += 1
            hold["xs_used"] = now("pe")
            if _ESTG <= 3:
                continue
            acg = now("ac")
            for fc in range(4):
                for hc in range(8):
                    w = []
                    if hc == 0:
                        w = [("ve", None), (wsem, pqw)] if fc == 0 \
                            else [("ac", acg)]
                    pe_op(lambda e, fc=fc, hc=hc, wb1=wb1: e.matmul(
                        ps_b[:, 0:CAP],
                        wb1[:, 512 * hc + 128 * fc:512 * hc + 128 * (fc + 1)],
                        xst[:, CAP * hc:CAP * (hc + 1)],
                        start=(hc == 0), stop=(hc == 7)), waits=w)
                ac_op(lambda e, fc=fc: e.activation(
                    out=h1g[:, CAP * fc:CAP * (fc + 1)], in_=ps_b[:, 0:CAP],
                    func=AF.Gelu_apprx_tanh), waits=[("pe", None)])
                acg = now("ac")
            if _ESTG <= 4:
                continue
            for r in range(3):
                for nh in range(2):
                    psX = ps_c if nh == 0 else ps_d
                    pskey = "yev_c" if nh == 0 else "yev_d"
                    for fc in range(4):
                        w = []
                        if fc == 0 and nh == 0 and r == 0:
                            w = [("ac", None)]
                        if fc == 0:
                            w = w + [("ve", hold.get(pskey, 0))]
                        pe_op(lambda e, r=r, nh=nh, fc=fc, psX=psX, wb2=wb2:
                              e.matmul(
                                  psX[:, :],
                                  h1g[:, CAP * fc + 128 * r:
                                       CAP * fc + 128 * (r + 1)],
                                  wb2[:, 1024 * fc + 512 * nh:
                                       1024 * fc + 512 * (nh + 1)],
                                  start=(fc == 0), stop=(fc == 3)), waits=w)
                    ve_op(lambda e, r=r, nh=nh, psX=psX: e.tensor_scalar(
                        y_sb[:, 512 * nh:512 * (nh + 1)], psX[:, :],
                        pw_f[:, r:r + 1], None, op0=ALU.mult),
                        waits=[("pe", None), ("ly", None)])
                    hold[pskey] = now("ve")
                dma("sync", "ly",
                    y_buf[CAP * e_i + 128 * r:CAP * e_i + 128 * (r + 1), :],
                    y_sb[:, :], waits=[("ve", None)])
            buf_last_use[e_i % 2] = now("pe")
        sch.barrier()

        if _PHASE <= 7:
            _truncate()
            return nc

        s_p5.close()
        open_scopes.remove(s_p5)
        s_p6 = ExitStack()
        open_scopes.append(s_p6)
        sw1t = s_p6.enter_context(
            nc.sbuf_tensor("sw1t", [128, 8 * SF], BF16))
        sw2t = s_p6.enter_context(
            nc.sbuf_tensor("sw2t", [128, 16 * H], BF16))
        shp = s_p6.enter_context(nc.sbuf_tensor("shp", [128, 8 * 512], BF16))
        h1gs = s_p6.enter_context(
            nc.sbuf_tensor("h1gs", [128, 16 * 512], BF16))
        hi_own = s_p6.enter_context(
            nc.sbuf_tensor("hi_own", [128, 4 * H], BF16))
        moe_t = s_p6.enter_context(nc.sbuf_tensor("moe_t", [128, H], F32))
        xm_t = s_p6.enter_context(nc.sbuf_tensor("xm_t", [128, H], F32))
        out_t = s_p6.enter_context(nc.sbuf_tensor("out_t", [128, H], F32))
        for c in range(8):
            dmat(sw1t[:, SF * c:SF * (c + 1)],
                 sw1_bfd[0:SF, 128 * c:128 * (c + 1)],
                 waits=[("pq", None), ("ly", None), ("ld", None),
                        ("pe", None), ("ve", None)] if c == 0 else ())
        for c in range(16):
            dmat(sw2t[:, H * c:H * (c + 1)],
                 sw2_bfd[0:H, 128 * c:128 * (c + 1)])
        sch.op("gpsimd", lambda e: e.nop(),
               waits=[("ld", None), ("pq", None), ("ly", None),
                      ("lo", None), ("lg", None)])
        sch.op("tensor", lambda e: e.nop(), waits=[("sq", None)])
        for c4 in range(4):
            for j in range(4):
                sch.op("gpsimd", lambda e, c4=c4, j=j: e.indirect_dma_start(
                    out=hi_own[:, H * j:H * (j + 1)], out_offset=None,
                    in_=xln2_bf,
                    in_offset=IndirectOffsetOnAxis(
                        ap=oidx[:, 4 * c4 + j:4 * c4 + j + 1], axis=0)),
                    incs=[("pq", 16)],
                    waits=[("pe", hold.get("shp_used", 0))] if j == 0 else ())
            pqw = now("pq")
            po_op(lambda e: e.nop(), waits=[("pq", pqw)])
            pow_ = now("po")
            psdb6 = ps_d[:, 0:256].bitcast(BF16)
            tcnt = 0
            for j in range(4):
                for hc in range(8):
                    dstv = (ps_t[:, 0:128] if tcnt % 2 == 0
                            else psdb6[:, 0:128])
                    w = [("po", pow_), ("ve", None)] if tcnt == 0 else ()
                    if tcnt >= 2:
                        w = [("ve", hold.get(("sev", c4, tcnt - 2), 0))]
                    pe_op(lambda e, j=j, hc=hc, dstv=dstv: e.transpose(
                        dstv,
                        hi_own[:, H * j + 128 * hc:H * j + 128 * (hc + 1)],
                        idt_b[:, :]), waits=w)
                    ve_op(lambda e, j=j, hc=hc, dstv=dstv: e.tensor_copy(
                        out=shp[:, 512 * hc + 128 * j:
                                512 * hc + 128 * (j + 1)],
                        in_=dstv),
                        waits=[("pe", None)])
                    hold[("sev", c4, tcnt)] = now("ve")
                    tcnt += 1
            acg0 = now("ac")
            for fc in range(16):
                for hc in range(8):
                    w = []
                    if hc == 0:
                        w = [("ve", None)] if fc == 0 else [("ac", acg0)]
                    pe_op(lambda e, fc=fc, hc=hc: e.matmul(
                        ps_b[:, :],
                        sw1t[:, SF * hc + 128 * fc:SF * hc + 128 * (fc + 1)],
                        shp[:, 512 * hc:512 * (hc + 1)],
                        start=(hc == 0), stop=(hc == 7)), waits=w)
                ac_op(lambda e, fc=fc: e.activation(
                    out=h1gs[:, 512 * fc:512 * (fc + 1)], in_=ps_b[:, :],
                    func=AF.Gelu_apprx_tanh), waits=[("pe", None)])
                acg0 = now("ac")
            hold["shp_used"] = now("pe")
            for j in range(4):
                gown = 4 * c4 + j
                sch.op("gpsimd", lambda e, c4=c4, j=j: e.indirect_dma_start(
                    out=gsl[:, :], out_offset=None,
                    in_=gs_sp,
                    in_offset=IndirectOffsetOnAxis(
                        ap=oidx[:, 4 * c4 + j:4 * c4 + j + 1], axis=0)),
                    incs=[("pq", 16)],
                    waits=[("ve", hold.get("out_done", 0))])
                pq_gsl = now("pq")
                for k in range(TOPK):
                    cop = ALU.bypass if k == 0 else ALU.add
                    sch.op("gpsimd", lambda e, k=k, cop=cop:
                           e.indirect_dma_start(
                               out=moe_t[:, :], out_offset=None,
                               in_=y_buf,
                               in_offset=IndirectOffsetOnAxis(
                                   ap=gsl[:, k:k + 1], axis=0),
                               compute_op=cop),
                           incs=[("pq", 16)],
                           waits=[("pq", pq_gsl)] if k == 0
                           else [("pq", None)])
                sch.op("gpsimd", lambda e, c4=c4, j=j: e.indirect_dma_start(
                    out=xm_t[:, :], out_offset=None,
                    in_=xmid_sp,
                    in_offset=IndirectOffsetOnAxis(
                        ap=oidx[:, 4 * c4 + j:4 * c4 + j + 1], axis=0)),
                    incs=[("pq", 16)], waits=[("pq", None)])
                po_op(lambda e: e.nop(), waits=[("pq", None)])
                pot = now("po")
                for nh in range(2):
                    psX = ps_c if nh == 0 else ps_d
                    pskey = "oev_c" if nh == 0 else "oev_d"
                    for fc in range(16):
                        w = []
                        if fc == 0:
                            w = [("ac", None),
                                 ("ve", hold.get(pskey, 0))]
                        pe_op(lambda e, j=j, nh=nh, fc=fc, psX=psX: e.matmul(
                            psX[:, :],
                            h1gs[:, 512 * fc + 128 * j:
                                 512 * fc + 128 * (j + 1)],
                            sw2t[:, 1024 * fc + 512 * nh:
                                 1024 * fc + 512 * (nh + 1)],
                            start=(fc == 0), stop=(fc == 15)), waits=w)
                    ve_op(lambda e, nh=nh, psX=psX: e.tensor_tensor(
                        out_t[:, 512 * nh:512 * (nh + 1)], psX[:, :],
                        moe_t[:, 512 * nh:512 * (nh + 1)], op=ALU.add),
                        waits=[("pe", None), ("po", pot), ("ld", None)])
                    ve_op(lambda e, nh=nh: e.tensor_tensor(
                        out_t[:, 512 * nh:512 * (nh + 1)],
                        out_t[:, 512 * nh:512 * (nh + 1)],
                        xm_t[:, 512 * nh:512 * (nh + 1)], op=ALU.add),
                        waits=[("ve", None)])
                    hold[pskey] = now("ve")
                dma("sync", "ld", out_ext[128 * gown:128 * (gown + 1), :],
                    out_t[:, :], waits=[("ve", None)])
                hold["out_done"] = now("ve")
        sch.op("sync", lambda e: e.nop(),
               waits=[("ld", None), ("pq", None), ("sq", None),
                      ("lx", None), ("lo", None), ("lg", None),
                      ("ly", None), ("s0", None), ("s1", None),
                      ("w0", None), ("w1", None)])
        s_p6.close()
        open_scopes.remove(s_p6)
        block = ctx.enter_context(nc.Block())
        sch.emit(block)
    return nc


_NC_CACHE = {}


def _build_cached():
    if "nc" not in _NC_CACHE:
        _NC_CACHE["nc"] = build()
    return _NC_CACHE["nc"]


def kernel(L0_attn_x, L0_attn_Wq, L0_attn_Wk, L0_attn_Wv,
           L0_ffn_router_weight, L0_ffn_expert_bias, L0_ffn_w1, L0_ffn_w2,
           L0_ffn_sw1, L0_ffn_sw2):
    x2d = np.ascontiguousarray(np.asarray(L0_attn_x, np.float32).reshape(T, H))
    wq = np.ascontiguousarray(np.asarray(L0_attn_Wq, np.float32))
    wk = np.ascontiguousarray(np.asarray(L0_attn_Wk, np.float32))
    wv = np.ascontiguousarray(np.asarray(L0_attn_Wv, np.float32))
    rw = np.ascontiguousarray(np.asarray(L0_ffn_router_weight, np.float32))
    eb = np.ascontiguousarray(np.asarray(L0_ffn_expert_bias,
                                         np.float32).reshape(1, E))
    w1 = np.ascontiguousarray(np.asarray(L0_ffn_w1,
                                         np.float32).reshape(E * H, FE))
    w2 = np.ascontiguousarray(np.asarray(L0_ffn_w2,
                                         np.float32).reshape(E * FE, H))
    sw1 = np.ascontiguousarray(np.asarray(L0_ffn_sw1, np.float32))
    sw2 = np.ascontiguousarray(np.asarray(L0_ffn_sw2, np.float32))

    nc = _build_cached()
    in_maps = []
    for c in range(NCORE):
        tiles = np.arange(c * NTPC, (c + 1) * NTPC, dtype=np.uint32)
        oidx_np = (tiles[None, :] * 128
                   + np.arange(128, dtype=np.uint32)[:, None])
        in_maps.append({
            "x_in": x2d, "wq_in": wq, "wk_in": wk, "wv_in": wv,
            "rw_in": rw, "bias_in": eb, "w1_in": w1, "w2_in": w2,
            "sw1_in": sw1, "sw2_in": sw2,
            "oidx_in": np.ascontiguousarray(oidx_np),
        })
    res = run_bass_kernel_spmd(nc, in_maps, core_ids=list(range(NCORE)))
    out = np.concatenate([res.results[c]["out_ext"] for c in range(NCORE)],
                         axis=0)
    return out.reshape(B, SEQ, H)


# version 15
# speedup vs baseline: 1.3024x; 1.3024x over previous
import os
import numpy as np
import ml_dtypes
from contextlib import ExitStack

_PHASE = int(os.environ.get("KPHASE", "99"))
_ESTG = int(os.environ.get("KEXPSTAGE", "9"))

import concourse.bass as bass
import concourse.mybir as mybir
from concourse.bass import IndirectOffsetOnAxis
from concourse.bass_utils import run_bass_kernel_spmd

F32 = mybir.dt.float32
BF16 = mybir.dt.bfloat16
U32 = mybir.dt.uint32
I32 = mybir.dt.int32
AF = mybir.ActivationFunctionType
ALU = mybir.AluOpType
AX = mybir.AxisListType

B, SEQ, H, D = 4, 4096, 1024, 64
T = B * SEQ
E, TOPK, FE = 64, 6, 512
CAP = 384
NSLOT = E * CAP
SF = 2048
NT = T // 128
TPC = T // 8
NTPC = TPC // 128
EXP_SHIFT = -20.0
NCORE = 8

ENGS = ("sync", "scalar", "vector", "tensor", "gpsimd")


class Sched:

    def __init__(self, nc, ctx):
        self.nc = nc
        self.thunks = {e: [] for e in ENGS}
        self.sems = {}
        self.cnt = {}
        self.ctx = ctx

    def sem(self, name):
        self.sems[name] = self.ctx.enter_context(self.nc.semaphore(name))
        self.cnt[name] = 0

    def op(self, eng, emit, incs=(), waits=()):
        wlist = [(s, self.cnt[s] if v is None else v) for s, v in waits]
        ilist = list(incs)
        for s, n in ilist:
            self.cnt[s] += n
        sems = self.sems

        def thunk(e):
            for s, v in wlist:
                if v > 0:
                    e.wait_ge(sems[s], v)
            ins = emit(e)
            for s, n in ilist:
                ins.then_inc(sems[s], n)
        self.thunks[eng].append(thunk)

    def barrier(self):
        target = self.cnt["ph"] + len(ENGS)
        for e in ENGS:
            self.op(e, lambda en: en.nop(), incs=[("ph", 1)])
        for e in ENGS:
            self.op(e, lambda en: en.nop(), waits=[("ph", target)])

    def emit(self, block):
        for e in ENGS:
            ths = self.thunks[e]

            def runner(eng, ths=ths):
                for t_ in ths:
                    t_(eng)
            getattr(block, e)(runner)


def build():
    nc = bass.Bass("TRN2", target_bir_lowering=False, enable_asserts=False,
                   dynamic_dma_scratch_size=32768)

    def dt(name, shape, dtype, kind="Internal"):
        return nc.dram_tensor(name, shape, dtype, kind=kind)

    x_in = dt("x_in", [T, H], F32, "ExternalInput").ap()
    wq_in = dt("wq_in", [H, D], F32, "ExternalInput").ap()
    wk_in = dt("wk_in", [H, D], F32, "ExternalInput").ap()
    wv_in = dt("wv_in", [H, D], F32, "ExternalInput").ap()
    rw_in = dt("rw_in", [E, H], F32, "ExternalInput").ap()
    bias_in = dt("bias_in", [1, E], F32, "ExternalInput").ap()
    w1_in = dt("w1_in", [E * H, FE], F32, "ExternalInput").ap()
    w2_in = dt("w2_in", [E * FE, H], F32, "ExternalInput").ap()
    sw1_in = dt("sw1_in", [SF, H], F32, "ExternalInput").ap()
    sw2_in = dt("sw2_in", [H, SF], F32, "ExternalInput").ap()
    oidx_in = dt("oidx_in", [128, NTPC], U32, "ExternalInput").ap()
    out_ext = dt("out_ext", [TPC, H], F32, "ExternalOutput").ap()

    xln1_bf = dt("xln1_bf", [T, H], BF16).ap()
    xln2_bf = dt("xln2_bf", [T, H], BF16).ap()
    xln2_lo = dt("xln2_lo", [T, H], BF16).ap()
    sw1_bfd = dt("sw1_bfd", [SF, H], BF16).ap()
    sw2_bfd = dt("sw2_bfd", [H, SF], BF16).ap()
    ot_buf = dt("ot_buf", [B * D, SEQ], F32).ap()
    xmid_sp = dt("xmid_sp", [T, H], F32).ap()
    gs_sp = dt("gs_sp", [T, 8], U32).ap()
    xs_buf = dt("xs_buf", [NSLOT, 1024], BF16).ap()
    y_buf = dt("y_buf", [NSLOT + 128, H], BF16).ap()
    pw_sp = dt("pw_sp", [T, 6], BF16).ap()

    with ExitStack() as ctx:
        def sb(name, shape, dtype):
            return ctx.enter_context(nc.sbuf_tensor(name, shape, dtype))

        def ptens(name, shape, dtype):
            return ctx.enter_context(nc.psum_tensor(name, shape, dtype))

        wq_b = sb("wq_b", [128, 8 * D], BF16)
        wk_b = sb("wk_b", [128, 8 * D], BF16)
        wv_b = sb("wv_b", [128, 8 * D], BF16)
        rwt_hi = sb("rwt_hi", [128, 8 * E], BF16)
        rwt_lo = sb("rwt_lo", [128, 8 * E], BF16)
        bias_r = sb("bias_r", [1, E], F32)
        bias_bc = sb("bias_bc", [128, E], F32)
        idt_b = sb("idt_b", [128, 128], BF16)
        idt_f = sb("idt_f", [128, 128], F32)
        tri_b = sb("tri_b", [128, 128], BF16)
        ones_c = sb("ones_c", [128, 1], BF16)
        ones_r1 = sb("ones_r1", [1, 128], F32)
        iota128 = sb("iota128", [128, 128], F32)
        iota128i = sb("iota128i", [128, 128], I32)
        iotap = sb("iotap", [128, 1], F32)
        iotapi = sb("iotapi", [128, 1], I32)
        inval6 = sb("inval6", [128, 6], F32)
        oidx = sb("oidx", [128, NTPC], U32)
        i6_all = sb("i6_all", [128, NT * 6], F32)
        pw_all = sb("pw_all", [128, NT * 6], BF16)
        sl_all = sb("sl_all", [128, NT * 6], U32)
        ln_sq = sb("ln_sq", [128, H], F32)
        ln_s = sb("ln_s", [128, 8], F32)
        mvtW = sb("mvtW", [128, 2048], BF16)
        mvtW2 = sb("mvtW2", [128, 2048], BF16)
        gs_u = sb("gs_u", [128, 8], U32)
        zed = sb("zed", [128, H], BF16)
        gsl = sb("gsl", [128, 8], U32)
        rwt_f = sb("rwt_f", [128, E], F32)
        rwt_l = sb("rwt_l", [128, E], F32)
        eps_c = sb("eps_c", [128, 1], F32)
        shf_c = sb("shf_c", [128, 1], F32)
        ps_a = ptens("ps_a", [128, 2048], F32)
        ps_b = ptens("ps_b", [128, 512], F32)
        ps_c = ptens("ps_c", [128, 512], F32)
        ps_d = ptens("ps_d", [128, 512], F32)
        ps_t = ptens("ps_t", [128, 512], BF16)

        sch = Sched(nc, ctx)
        for s_ in ["ld", "sq", "pq", "pe", "ve", "ac", "po", "ph", "ta", "tb",
                   "lx", "lo", "lg", "ly", "s0", "s1", "w0", "w1"]:
            sch.sem(s_)
        now = lambda s_: sch.cnt[s_]
        hold = {}
        open_scopes = []

        def ve_op(emit, waits=()):
            sch.op("vector", emit, incs=[("ve", 1)], waits=waits)

        def ac_op(emit, waits=()):
            sch.op("scalar", emit, incs=[("ac", 1)], waits=waits)

        def pe_op(emit, waits=()):
            sch.op("tensor", emit, incs=[("pe", 1)], waits=waits)

        def po_op(emit, waits=()):
            sch.op("gpsimd", emit, incs=[("po", 1)], waits=waits)

        def dma(eng, sem, out_ap, in_ap, waits=()):
            sch.op(eng, lambda e: e.dma_start(out=out_ap, in_=in_ap),
                   incs=[(sem, 16)], waits=waits)

        def dmat(out_ap, in_ap, waits=()):
            sch.op("scalar", lambda e: e.dma_start_transpose(out_ap, in_ap),
                   incs=[("sq", 16)], waits=waits)

        s_p0 = ExitStack()
        open_scopes.append(s_p0)
        wst = s_p0.enter_context(nc.sbuf_tensor("wst", [128, 1024], F32))
        wst2 = s_p0.enter_context(nc.sbuf_tensor("wst2", [128, 512], F32))
        rw_st = s_p0.enter_context(nc.sbuf_tensor("rw_st", [64, H], F32))
        dma("sync", "ld", wst[:, 0:512].rearrange("p (c d) -> p c d", c=8),
            wq_in.rearrange("(c p) d -> p c d", p=128))
        dma("sync", "ld",
            wst[:, 512:1024].rearrange("p (c d) -> p c d", c=8),
            wk_in.rearrange("(c p) d -> p c d", p=128))
        dma("sync", "ld", wst2[:, 0:512].rearrange("p (c d) -> p c d", c=8),
            wv_in.rearrange("(c p) d -> p c d", p=128))
        dma("sync", "ld", rw_st[:, :], rw_in)
        dma("sync", "ld", bias_r[:, :], bias_in)
        dma("sync", "ld", oidx[:, :], oidx_in)
        po_op(lambda e: e.iota(iotapi[:], pattern=[[0, 1]], base=0,
                               channel_multiplier=1))
        po_op(lambda e: e.iota(iota128i[:], pattern=[[1, 128]], base=0,
                               channel_multiplier=0))

        ve_op(lambda e: e.tensor_copy(out=iotap[:], in_=iotapi[:]),
              waits=[("po", None), ("ld", None)])
        ve_op(lambda e: e.tensor_copy(out=iota128[:], in_=iota128i[:]))
        ve_op(lambda e: e.tensor_scalar(idt_f[:], iota128[:], iotap[:, 0:1],
                                        None, op0=ALU.is_equal),
              waits=[("ve", None)])
        ve_op(lambda e: e.tensor_scalar(tri_b[:], iota128[:], iotap[:, 0:1],
                                        None, op0=ALU.is_gt))
        ve_op(lambda e: e.tensor_copy(out=idt_b[:], in_=idt_f[:]),
              waits=[("ve", None)])
        ve_op(lambda e: e.memset(ones_c[:], 1.0))
        ve_op(lambda e: e.memset(ones_r1[:], 1.0))
        ve_op(lambda e: e.memset(inval6[:], 0.0))
        ve_op(lambda e: e.tensor_scalar(inval6[:], inval6[:], iotap[:, 0:1],
                                        100000.0, op0=ALU.add, op1=ALU.add),
              waits=[("ve", None)])
        ve_op(lambda e: e.tensor_copy(out=wq_b[:], in_=wst[:, 0:512]))
        ve_op(lambda e: e.tensor_copy(out=wk_b[:], in_=wst[:, 512:1024]))
        ve_op(lambda e: e.tensor_copy(out=wv_b[:], in_=wst2[:, 0:512]))
        ve_op(lambda e: e.memset(zed[:], 0.0))
        ve_op(lambda e: e.memset(eps_c[:], 1e-5))
        ve_op(lambda e: e.memset(shf_c[:], EXP_SHIFT))
        ve_op(lambda e: e.memset(gs_u[:, 6:8], NSLOT))

        for c in range(8):
            pe_op(lambda e, c=c: e.transpose(
                ps_d[:, 0:64], rw_st[:, 128 * c:128 * (c + 1)],
                idt_f[0:64, 0:64]), waits=[("ve", None)])
            ve_op(lambda e: e.tensor_copy(out=rwt_f[:, :], in_=ps_d[:, 0:E]),
                  waits=[("pe", None)])
            ve_op(lambda e, c=c: e.tensor_copy(
                out=rwt_hi[:, E * c:E * (c + 1)], in_=rwt_f[:, :]),
                waits=[("ve", None)])
            ve_op(lambda e, c=c: e.tensor_tensor(
                rwt_l[:, :], rwt_f[:, :], rwt_hi[:, E * c:E * (c + 1)],
                op=ALU.subtract), waits=[("ve", None)])
            ve_op(lambda e, c=c: e.tensor_copy(
                out=rwt_lo[:, E * c:E * (c + 1)], in_=rwt_l[:, :]),
                waits=[("ve", None)])
        pe_op(lambda e: e.matmul(ps_c[0:128, 0:E], ones_r1[:, :], bias_r[:, :],
                                 start=True, stop=True),
              waits=[("ve", None)])
        ve_op(lambda e: e.tensor_copy(out=bias_bc[:], in_=ps_c[0:128, 0:E]),
              waits=[("pe", None)])

        dma("sync", "ld", y_buf[NSLOT:NSLOT + 128, :], zed[:],
            waits=[("ve", None)])
        sch.barrier()
        s_p0.close()
        open_scopes.remove(s_p0)


        def _truncate():
            for s in reversed(open_scopes):
                s.close()
            for gown in range(NTPC):
                dma("sync", "ld", out_ext[128 * gown:128 * (gown + 1), :],
                    ln_sq[:, :], waits=[("ve", None)])
            sch.op("sync", lambda e: e.nop(),
                   waits=[("ld", None), ("pq", None), ("sq", None),
                          ("lx", None), ("lo", None), ("lg", None),
                          ("ly", None), ("s0", None), ("s1", None),
                          ("w0", None), ("w1", None)])
            block = ctx.enter_context(nc.Block())
            sch.emit(block)

        if _PHASE <= 0:
            _truncate()
            return nc

        def emit_ln(src_ap, dst_f32, dst_b16=None, pre_waits=()):
            ve_op(lambda e: e.reduce_sum(ln_s[:, 0:1], src_ap, axis=AX.X),
                  waits=list(pre_waits) + [("ac", None)])
            ac_op(lambda e: e.activation(out=ln_sq[:, :], in_=src_ap,
                                         func=AF.Square,
                                         accum_out=ln_s[:, 1:2]),
                  waits=list(pre_waits) + [("ve", None)])
            ve_op(lambda e: e.tensor_scalar(ln_s[:, 2:3], ln_s[:, 0:1],
                                            1.0 / H, None, op0=ALU.mult),
                  waits=[("ve", None)])
            ve_op(lambda e: e.tensor_scalar(ln_s[:, 3:4], ln_s[:, 2:3],
                                            ln_s[:, 2:3], None, op0=ALU.mult),
                  waits=[("ve", None)])
            ve_op(lambda e: e.tensor_scalar(ln_s[:, 4:5], ln_s[:, 1:2],
                                            1.0 / H, None, op0=ALU.mult),
                  waits=[("ac", None)])
            ve_op(lambda e: e.tensor_tensor(ln_s[:, 5:6], ln_s[:, 4:5],
                                            ln_s[:, 3:4], op=ALU.subtract),
                  waits=[("ve", None)])
            ac_op(lambda e: e.activation(out=ln_s[:, 6:7], in_=ln_s[:, 5:6],
                                         func=AF.Sqrt, bias=eps_c[:, 0:1], scale=1.0),
                  waits=[("ve", None)])
            ve_op(lambda e: e.reciprocal(ln_s[:, 7:8], ln_s[:, 6:7]),
                  waits=[("ac", None)])
            ve_op(lambda e: e.tensor_scalar(dst_f32, src_ap, ln_s[:, 2:3],
                                            ln_s[:, 7:8], op0=ALU.subtract,
                                            op1=ALU.mult),
                  waits=[("ve", None)])
            if dst_b16 is not None:
                ve_op(lambda e: e.tensor_copy(out=dst_b16, in_=dst_f32),
                      waits=[("ve", None)])

        s_ln = ExitStack()
        open_scopes.append(s_ln)
        xstg = s_ln.enter_context(nc.sbuf_tensor("xstg", [128, 4 * H], F32))
        ostg = s_ln.enter_context(nc.sbuf_tensor("ostg", [128, 4 * H], F32))
        hstg = s_ln.enter_context(nc.sbuf_tensor("hstg", [128, 4 * H], BF16))
        lstg = s_ln.enter_context(nc.sbuf_tensor("lstg", [128, 4 * H], BF16))
        otok = s_ln.enter_context(nc.sbuf_tensor("otok", [128, D], F32))
        ost64 = s_ln.enter_context(nc.sbuf_tensor("ost64", [64, 128], F32))
        NG = NT // 4
        for g in range(NG):
            for j in range(4):
                tile = 4 * g + j
                eng_l, sem_l = (("sync", "lx") if j < 2
                                else ("gpsimd", "pq"))
                dma(eng_l, sem_l, xstg[:, H * j:H * (j + 1)],
                    x_in[128 * tile:128 * (tile + 1), :],
                    waits=[("ta", g)] if (j in (0, 2) and g >= 1) else ())
            ldw = now("lx")
            pqw = now("pq")
            for j in range(4):
                emit_ln(xstg[:, H * j:H * (j + 1)],
                        ostg[:, H * j:H * (j + 1)],
                        hstg[:, H * j:H * (j + 1)],
                        pre_waits=[("lx", ldw), ("pq", pqw)]
                        if j == 0 else ())
            sch.op("scalar", lambda e, g=g: e.dma_start(
                out=xln1_bf[512 * g:512 * (g + 1), :].rearrange(
                    "(j p) h -> p j h", p=128),
                in_=hstg[:, :].rearrange("p (j h) -> p j h", j=4)),
                incs=[("s1", 16)],
                waits=[("ve", None), ("ac", None)])
            sch.op("vector", lambda e: e.nop(), incs=[("ta", 1)],
                   waits=[("s1", now("s1")), ("ac", None)])
        sch.barrier()

        if _PHASE <= 1:
            _truncate()
            return nc

        s_att = ExitStack()
        open_scopes.append(s_att)
        qt = s_att.enter_context(nc.sbuf_tensor("qt", [64, SEQ], BF16))
        kt = s_att.enter_context(nc.sbuf_tensor("kt", [64, SEQ], BF16))
        vt = s_att.enter_context(nc.sbuf_tensor("vt", [64, SEQ], BF16))
        tspan = s_att.enter_context(
            nc.sbuf_tensor("tspan", [128, 8 * 2048], BF16))
        at_t0 = s_att.enter_context(nc.sbuf_tensor("at_t0", [128, SEQ], BF16))
        at_t1 = s_att.enter_context(nc.sbuf_tensor("at_t1", [128, SEQ], BF16))
        rs8 = s_att.enter_context(nc.sbuf_tensor("rs8", [128, 8], F32))
        rsum = s_att.enter_context(nc.sbuf_tensor("rsum", [128, 1], F32))
        vs_t0 = s_att.enter_context(nc.sbuf_tensor("vs_t0", [128, D], BF16))
        vs_t1 = s_att.enter_context(nc.sbuf_tensor("vs_t1", [128, D], BF16))
        osb = s_att.enter_context(nc.sbuf_tensor("osb", [128, SEQ], F32))
        osb2 = s_att.enter_context(nc.sbuf_tensor("osb2", [64, SEQ], F32))
        for b in range(B):
            for th in range(2):
                rows = b * SEQ + th * 2048
                for hc in range(8):
                    dmat(tspan[:, 2048 * hc:2048 * (hc + 1)],
                         xln1_bf[rows:rows + 2048, 128 * hc:128 * (hc + 1)],
                         waits=[("pe", None), ("s1", None)] if hc == 0
                         else ())
                sqw = now("sq")
                for hc in range(8):
                    for nn in range(4):
                        w = [("sq", sqw), ("ve", None)] \
                            if (hc == 0 and nn == 0) else ()
                        sl = slice(2048 * hc + 512 * nn,
                                   2048 * hc + 512 * (nn + 1))
                        pe_op(lambda e, hc=hc, nn=nn, sl=sl: e.matmul(
                            ps_a[0:64, 512 * nn:512 * (nn + 1)],
                            wq_b[:, 64 * hc:64 * (hc + 1)], tspan[:, sl],
                            start=(hc == 0), stop=(hc == 7)), waits=w)
                        pe_op(lambda e, hc=hc, nn=nn, sl=sl: e.matmul(
                            ps_a[64:128, 512 * nn:512 * (nn + 1)],
                            wk_b[:, 64 * hc:64 * (hc + 1)], tspan[:, sl],
                            start=(hc == 0), stop=(hc == 7),
                            skip_group_check=True))
                ve_op(lambda e, th=th: e.tensor_copy(
                    out=qt[:, 2048 * th:2048 * (th + 1)], in_=ps_a[0:64, :]),
                    waits=[("pe", None)])
                ve_op(lambda e, th=th: e.tensor_copy(
                    out=kt[:, 2048 * th:2048 * (th + 1)], in_=ps_a[64:128, :]))
                vew = now("ve")
                for hc in range(8):
                    for nn in range(4):
                        w = [("ve", vew)] if (hc == 0 and nn == 0) else ()
                        sl = slice(2048 * hc + 512 * nn,
                                   2048 * hc + 512 * (nn + 1))
                        pe_op(lambda e, hc=hc, nn=nn, sl=sl: e.matmul(
                            ps_a[0:64, 512 * nn:512 * (nn + 1)],
                            wv_b[:, 64 * hc:64 * (hc + 1)], tspan[:, sl],
                            start=(hc == 0), stop=(hc == 7)), waits=w)
                ve_op(lambda e, th=th: e.tensor_copy(
                    out=vt[:, 2048 * th:2048 * (th + 1)], in_=ps_a[0:64, :]),
                    waits=[("pe", None)])
            ve_op(lambda e: e.memset(osb[:, :], 0.0),
                  waits=[("ve", None), ("ld", None)])
            ac_after_exp = [0] * 8
            for kp in range(16):
                for kk in range(2):
                    kt_i = 2 * kp + kk
                    at_t = at_t0 if kk == 0 else at_t1
                    vs_t = vs_t0 if kk == 0 else vs_t1
                    ve_after_rsum = now("ve")
                    for cc in range(8):
                        psS = ps_b if (cc % 2 == 0) else ps_c
                        w = []
                        if cc >= 2:
                            w.append(("ac", ac_after_exp[cc - 2]))
                        if cc == 0:
                            w.append(("ve", None))
                        pe_op(lambda e, kt_i=kt_i, cc=cc, psS=psS: e.matmul(
                            psS[:, :], kt[:, 128 * kt_i:128 * (kt_i + 1)],
                            qt[:, 512 * cc:512 * (cc + 1)],
                            start=True, stop=True), waits=w)
                        ac_op(lambda e, cc=cc, psS=psS, at_t=at_t:
                              e.activation(
                                  out=at_t[:, 512 * cc:512 * (cc + 1)],
                                  in_=psS[:, :], func=AF.Exp,
                                  bias=shf_c[:, 0:1], scale=1.0,
                                  accum_out=rs8[:, cc:cc + 1]),
                              waits=[("pe", None), ("ve", ve_after_rsum)])
                        ac_after_exp[cc] = now("ac")
                    ve_op(lambda e: e.reduce_sum(rsum[:, 0:1], rs8[:, :],
                                                 axis=AX.X),
                          waits=[("ac", None)])
                    ve_op(lambda e: e.reciprocal(rsum[:, 0:1], rsum[:, 0:1]),
                          waits=[("ve", None)])
                    pe_op(lambda e, kt_i=kt_i: e.transpose(
                        ps_t[:, 0:64], vt[:, 128 * kt_i:128 * (kt_i + 1)],
                        idt_b[0:64, 0:64]), waits=[("ve", None)])
                    ve_op(lambda e, vs_t=vs_t: e.tensor_scalar(
                        vs_t[:, :], ps_t[:, 0:64], rsum[:, 0:1], None,
                        op0=ALU.mult), waits=[("pe", None), ("ve", None)])
                vw = now("ve")
                for half in range(2):
                    for kk in range(2):
                        at_t = at_t0 if kk == 0 else at_t1
                        vs_t = vs_t0 if kk == 0 else vs_t1
                        for c2 in range(4):
                            w = []
                            if half == 0 and kk == 0 and c2 == 0:
                                w = [("ve", vw)]
                            if half == 1 and kk == 0 and c2 == 0:
                                w = [("ve", None)]
                            pe_op(lambda e, kk=kk, half=half, c2=c2,
                                  at_t=at_t, vs_t=vs_t: e.matmul(
                                      ps_a[64 * kk:64 * kk + 64,
                                           512 * c2:512 * (c2 + 1)],
                                      vs_t[:, :],
                                      at_t[:, 2048 * half + 512 * c2:
                                           2048 * half + 512 * (c2 + 1)],
                                      start=True, stop=True,
                                      skip_group_check=(kk == 1)), waits=w)
                    ve_op(lambda e, half=half: e.tensor_tensor(
                        osb[:, 2048 * half:2048 * (half + 1)],
                        osb[:, 2048 * half:2048 * (half + 1)],
                        ps_a[:, 0:2048], op=ALU.add),
                        waits=[("pe", None), ("ve", None)])
            dma("sync", "ld", osb2[:, :], osb[64:128, :],
                waits=[("ve", None)])
            ve_op(lambda e: e.tensor_tensor(osb[0:64, :], osb[0:64, :],
                                            osb2[:, :], op=ALU.add),
                  waits=[("ld", None)])
            dma("sync", "ld", ot_buf[D * b:D * (b + 1), :], osb[0:64, :],
                waits=[("ve", None)])
        sch.barrier()
        s_att.close()
        open_scopes.remove(s_att)

        if _PHASE <= 2:
            _truncate()
            return nc

        for g in range(NG):
            ldg0 = now("ld")
            for j in range(4):
                tile = 4 * g + j
                b = tile // 32
                col = (tile % 32) * 128
                dma("sync", "lx", xstg[:, H * j:H * (j + 1)],
                    x_in[128 * tile:128 * (tile + 1), :],
                    waits=([("tb", g)] if (j == 0 and g >= 1) else [])
                    + [("pe", None)])
                dma("sync", "lx", ost64[:, :],
                    ot_buf[D * b:D * (b + 1), col:col + 128],
                    waits=[("pe", None), ("ld", None)])
                ldw = now("lx")
                pe_op(lambda e: e.transpose(ps_d[:, 0:64], ost64[:, :],
                                            idt_f[0:64, 0:64]),
                      waits=[("lx", ldw), ("ve", None)])
                ve_op(lambda e: e.tensor_copy(out=otok[:, :],
                                              in_=ps_d[:, 0:64]),
                      waits=[("pe", None)])
                for r in range(16):
                    ve_op(lambda e, j=j, r=r: e.tensor_tensor(
                        xstg[:, H * j + 64 * r:H * j + 64 * (r + 1)],
                        xstg[:, H * j + 64 * r:H * j + 64 * (r + 1)],
                        otok[:, :], op=ALU.add),
                        waits=[("ve", None)] if r == 0 else ())
                emit_ln(xstg[:, H * j:H * (j + 1)],
                        ostg[:, H * j:H * (j + 1)],
                        hstg[:, H * j:H * (j + 1)],
                        pre_waits=[("ve", None)])
                ve_op(lambda e, j=j: e.tensor_tensor(
                    lstg[:, H * j:H * (j + 1)], ostg[:, H * j:H * (j + 1)],
                    hstg[:, H * j:H * (j + 1)], op=ALU.subtract),
                    waits=[("ve", None)])
            vew = now("ve")
            sch.op("gpsimd", lambda e, g=g: e.dma_start(
                out=xmid_sp[512 * g:512 * (g + 1), :].rearrange(
                    "(j p) h -> p j h", p=128),
                in_=xstg[:, :].rearrange("p (j h) -> p j h", j=4)),
                incs=[("lo", 16)],
                waits=[("ve", vew), ("ac", None)])
            sch.op("scalar", lambda e, g=g: e.dma_start(
                out=xln2_bf[512 * g:512 * (g + 1), :].rearrange(
                    "(j p) h -> p j h", p=128),
                in_=hstg[:, :].rearrange("p (j h) -> p j h", j=4)),
                incs=[("s0", 16)],
                waits=[("ve", vew), ("ac", None)])
            sch.op("scalar", lambda e, g=g: e.dma_start(
                out=xln2_lo[512 * g:512 * (g + 1), :].rearrange(
                    "(j p) h -> p j h", p=128),
                in_=lstg[:, :].rearrange("p (j h) -> p j h", j=4)),
                incs=[("s1", 16)],
                waits=[("ve", vew), ("ac", None)])
            sch.op("vector", lambda e: e.nop(), incs=[("tb", 1)],
                   waits=[("lo", now("lo")), ("s0", now("s0")),
                          ("s1", now("s1"))])
        sch.barrier()
        if _PHASE <= 3:
            _truncate()
            return nc
        s_ln.close()
        open_scopes.remove(s_ln)
        s_p3 = ExitStack()
        open_scopes.append(s_p3)
        mm_all = s_p3.enter_context(
            nc.sbuf_tensor("mm_all", [128, NT * E], BF16))
        lgT = s_p3.enter_context(nc.sbuf_tensor("lgT", [64, 2048], F32))
        sc_t = s_p3.enter_context(nc.sbuf_tensor("sc_t", [128, E], F32))
        scb_t = s_p3.enter_context(nc.sbuf_tensor("scb_t", [128, E], F32))
        m8 = s_p3.enter_context(nc.sbuf_tensor("m8", [128, 8], F32))
        i8 = s_p3.enter_context(nc.sbuf_tensor("i8", [128, 8], U32))
        i8f = s_p3.enter_context(nc.sbuf_tensor("i8f", [128, 8], F32))
        cmp6 = s_p3.enter_context(nc.sbuf_tensor("cmp6", [128, 384], F32))
        prod6 = s_p3.enter_context(nc.sbuf_tensor("prod6", [128, 384], F32))
        ext6 = s_p3.enter_context(nc.sbuf_tensor("ext6", [128, 6], F32))
        pos6 = s_p3.enter_context(nc.sbuf_tensor("pos6", [128, 6], F32))
        pw6 = s_p3.enter_context(nc.sbuf_tensor("pw6", [128, 6], F32))
        slot6 = s_p3.enter_context(nc.sbuf_tensor("slot6", [128, 6], F32))
        vm6 = s_p3.enter_context(nc.sbuf_tensor("vm6", [128, 6], F32))
        nv6 = s_p3.enter_context(nc.sbuf_tensor("nv6", [128, 6], F32))
        tt1 = s_p3.enter_context(nc.sbuf_tensor("tt1", [128, 6], F32))
        tt2 = s_p3.enter_context(nc.sbuf_tensor("tt2", [128, 6], F32))
        sl_f = s_p3.enter_context(nc.sbuf_tensor("sl_f", [128, 6], F32))
        gs_f = s_p3.enter_context(nc.sbuf_tensor("gs_f", [128, 6], F32))
        sum6 = s_p3.enter_context(nc.sbuf_tensor("sum6", [128, 1], F32))
        mtree = s_p3.enter_context(nc.sbuf_tensor("mtree", [128, 192], F32))
        mfin = s_p3.enter_context(nc.sbuf_tensor("mfin", [128, 64], F32))
        pos_t = s_p3.enter_context(nc.sbuf_tensor("pos_t", [128, E], F32))
        cntT = s_p3.enter_context(nc.sbuf_tensor("cntT", [64, NT], F32))
        cnt_sc = s_p3.enter_context(nc.sbuf_tensor("cnt_sc", [64, NT], F32))
        cnt_ex = s_p3.enter_context(nc.sbuf_tensor("cnt_ex", [64, NT], F32))
        base_rows = s_p3.enter_context(
            nc.sbuf_tensor("base_rows", [128, E], F32))
        base_st = s_p3.enter_context(nc.sbuf_tensor("base_st", [1, E], F32))
        src_x = s_p3.enter_context(nc.sbuf_tensor("src_x", [128, 2 * H],
                                                  BF16))

        for tb in range(8):
            rows = 2048 * tb
            pewM = now("pe")
            for hc in range(8):
                buf = mvtW if hc % 2 == 0 else mvtW2
                psem = "s0" if hc % 2 == 0 else "s1"
                sch.op("scalar", lambda e, buf=buf, rows=rows, hc=hc:
                       e.dma_start_transpose(
                           buf[:, :],
                           xln2_bf[rows:rows + 2048,
                                   128 * hc:128 * (hc + 1)]),
                       incs=[(psem, 16)],
                       waits=[("pe", hold.get(("lgmm", tb, hc - 2), 0)),
                              ("ve", None), ("s0", None)] if hc < 2 else
                             [("pe", hold.get(("lgmm", tb, hc - 2), 0))])
                sqw = now(psem)
                for cc in range(4):
                    w = [(psem, sqw)] if cc == 0 else ()
                    pe_op(lambda e, hc=hc, cc=cc, buf=buf: e.matmul(
                        ps_a[0:64, 512 * cc:512 * (cc + 1)],
                        rwt_hi[:, E * hc:E * (hc + 1)],
                        buf[:, 512 * cc:512 * (cc + 1)],
                        start=(hc == 0), stop=False), waits=w)
                    pe_op(lambda e, hc=hc, cc=cc, buf=buf: e.matmul(
                        ps_a[0:64, 512 * cc:512 * (cc + 1)],
                        rwt_lo[:, E * hc:E * (hc + 1)],
                        buf[:, 512 * cc:512 * (cc + 1)],
                        start=False, stop=False))
                hold[("lgmm", tb, hc)] = now("pe")
            for hc in range(8):
                buf = mvtW if hc % 2 == 0 else mvtW2
                psem = "s0" if hc % 2 == 0 else "s1"
                sch.op("scalar", lambda e, buf=buf, rows=rows, hc=hc:
                       e.dma_start_transpose(
                           buf[:, :],
                           xln2_lo[rows:rows + 2048,
                                   128 * hc:128 * (hc + 1)]),
                       incs=[(psem, 16)],
                       waits=[("pe", hold.get(("lgml", tb, hc - 2),
                                              hold[("lgmm", tb, 7)]))] +
                             ([("s1", None)] if (tb == 0 and hc == 0)
                              else []))
                sqw = now(psem)
                for cc in range(4):
                    w = [(psem, sqw)] if cc == 0 else ()
                    pe_op(lambda e, hc=hc, cc=cc, buf=buf: e.matmul(
                        ps_a[0:64, 512 * cc:512 * (cc + 1)],
                        rwt_hi[:, E * hc:E * (hc + 1)],
                        buf[:, 512 * cc:512 * (cc + 1)],
                        start=False, stop=(hc == 7)), waits=w)
                hold[("lgml", tb, hc)] = now("pe")
            ve_op(lambda e: e.tensor_copy(out=lgT[:, :], in_=ps_a[0:64, :]),
                  waits=[("pe", None)])
            for j16 in range(16):
                g = 16 * tb + j16
                pe_op(lambda e, j16=j16: e.transpose(
                    ps_d[:, 0:64], lgT[:, 128 * j16:128 * (j16 + 1)],
                    idt_f[0:64, 0:64]),
                    waits=[("ve", None), ("ac", None)])
                ac_op(lambda e: e.activation(out=sc_t[:, :],
                                             in_=ps_d[:, 0:64],
                                             func=AF.Sigmoid),
                      waits=[("pe", None)])
                ve_op(lambda e: e.tensor_tensor(scb_t[:, :], sc_t[:, :],
                                                bias_bc[:, :], op=ALU.add),
                      waits=[("ac", None), ("ve", None)])
                ve_op(lambda e: e.max(out=m8[:, :], in_=scb_t[:, :]),
                      waits=[("ve", None)])
                ve_op(lambda e: e.max_index(out=i8[:, :], in_max=m8[:, :],
                                            in_values=scb_t[:, :]),
                      waits=[("ve", None)])
                ve_op(lambda e: e.tensor_copy(out=i8f[:, :], in_=i8[:, :]),
                      waits=[("ve", None)])
                ve_op(lambda e, g=g: e.tensor_copy(
                    out=i6_all[:, 6 * g:6 * g + 6], in_=i8f[:, 0:6]),
                    waits=[("ve", None)])
                for k in range(TOPK):
                    ve_op(lambda e, k=k: e.tensor_scalar(
                        cmp6[:, 64 * k:64 * (k + 1)], iota128[:, 0:64],
                        i8f[:, k:k + 1], None, op0=ALU.is_equal),
                        waits=[("ve", None)] if k == 0 else ())
                for k in range(TOPK):
                    ve_op(lambda e, k=k: e.tensor_tensor(
                        prod6[:, 64 * k:64 * (k + 1)],
                        cmp6[:, 64 * k:64 * (k + 1)], sc_t[:, :],
                        op=ALU.mult),
                        waits=[("ve", None)] if k == 0 else ())
                for k in range(TOPK):
                    ve_op(lambda e, k=k: e.reduce_sum(
                        ext6[:, k:k + 1], prod6[:, 64 * k:64 * (k + 1)],
                        axis=AX.X), waits=[("ve", None)] if k == 0 else ())
                ve_op(lambda e: e.tensor_tensor(
                    mtree[:, 0:192], cmp6[:, 0:192], cmp6[:, 192:384],
                    op=ALU.add), waits=[("ve", None)])
                ve_op(lambda e: e.tensor_tensor(
                    mfin[:, :], mtree[:, 0:64], mtree[:, 64:128],
                    op=ALU.add), waits=[("ve", None)])
                ve_op(lambda e: e.tensor_tensor(
                    mfin[:, :], mfin[:, :], mtree[:, 128:192],
                    op=ALU.add), waits=[("ve", None)])
                ve_op(lambda e, g=g: e.tensor_copy(
                    out=mm_all[:, E * g:E * (g + 1)], in_=mfin[:, :]),
                    waits=[("ve", None)])
                ve_op(lambda e: e.reduce_sum(sum6[:, 0:1], ext6[:, :],
                                             axis=AX.X),
                      waits=[("ve", None)])
                ve_op(lambda e: e.tensor_scalar(sum6[:, 0:1], sum6[:, 0:1],
                                                1e-20, None, op0=ALU.add),
                      waits=[("ve", None)])
                ve_op(lambda e: e.reciprocal(sum6[:, 0:1], sum6[:, 0:1]),
                      waits=[("ve", None)])
                ve_op(lambda e: e.tensor_scalar(pw6[:, :], ext6[:, :],
                                                sum6[:, 0:1], None,
                                                op0=ALU.mult),
                      waits=[("ve", None)])
                ve_op(lambda e, g=g: e.tensor_copy(
                    out=pw_all[:, 6 * g:6 * g + 6], in_=pw6[:, :]),
                    waits=[("ve", None)])
                dma("sync", "lg", pw_sp[128 * g:128 * (g + 1), :],
                    pw_all[:, 6 * g:6 * g + 6], waits=[("ve", None)])
                pe_op(lambda e, g=g: e.matmul(
                    ps_c[0:64, 0:1], mm_all[:, E * g:E * (g + 1)],
                    ones_c[:, :], start=True, stop=True),
                    waits=[("ve", None)])
                ve_op(lambda e, g=g: e.tensor_copy(
                    out=cntT[:, g:g + 1], in_=ps_c[0:64, 0:1]),
                    waits=[("pe", None)])
        sch.barrier()

        if _PHASE <= 4:
            _truncate()
            return nc

        ve_op(lambda e: e.tensor_tensor_scan(
            cnt_sc[:, :], cntT[:, :], cntT[:, :], 0.0,
            op0=ALU.add, op1=ALU.bypass), waits=[("ve", None)])
        ve_op(lambda e: e.tensor_tensor(cnt_ex[:, :], cnt_sc[:, :],
                                        cntT[:, :], op=ALU.subtract),
              waits=[("ve", None)])
        pe_op(lambda e: e.transpose(ps_d[:, 0:64], cnt_ex[:, :],
                                     idt_f[0:64, 0:64]),
              waits=[("ve", None)])
        ve_op(lambda e: e.tensor_copy(out=base_rows[:, :],
                                      in_=ps_d[:, 0:64]),
              waits=[("pe", None)])
        for g in range(NT):
            w_re = ([("pq", hold[("scat", g - 2)])] if g >= 2
                    else [("s0", None)])
            dma("sync", "lx", src_x[:, H * (g % 2):H * (g % 2 + 1)],
                xln2_bf[128 * g:128 * (g + 1), :], waits=w_re)
            hold[("sxl", g)] = now("lx")
            dma("sync", "ld", base_st[:, :], base_rows[g:g + 1, :],
                waits=[("pe", None), ("ve", None)])
            ldw = now("ld")
            pe_op(lambda e, g=g: e.matmul(
                ps_c[0:128, 0:E], tri_b[:, :], mm_all[:, E * g:E * (g + 1)],
                start=True, stop=False), waits=[("ve", None)])
            pe_op(lambda e: e.matmul(ps_c[0:128, 0:E], ones_r1[:, :],
                                     base_st[:, :], start=False, stop=True),
                  waits=[("ld", ldw)])
            ve_op(lambda e: e.tensor_copy(out=pos_t[:, :],
                                          in_=ps_c[0:128, 0:E]),
                  waits=[("pe", None), ("ve", None)])
            for k in range(TOPK):
                ve_op(lambda e, k=k, g=g: e.tensor_scalar(
                    cmp6[:, 64 * k:64 * (k + 1)], iota128[:, 0:64],
                    i6_all[:, 6 * g + k:6 * g + k + 1], None,
                    op0=ALU.is_equal),
                    waits=[("ve", None)] if k == 0 else ())
            for k in range(TOPK):
                ve_op(lambda e, k=k: e.tensor_tensor(
                    prod6[:, 64 * k:64 * (k + 1)],
                    cmp6[:, 64 * k:64 * (k + 1)], pos_t[:, :], op=ALU.mult),
                    waits=[("ve", None)] if k == 0 else ())
            for k in range(TOPK):
                ve_op(lambda e, k=k: e.reduce_sum(
                    pos6[:, k:k + 1], prod6[:, 64 * k:64 * (k + 1)],
                    axis=AX.X), waits=[("ve", None)] if k == 0 else ())
            ve_op(lambda e, g=g: e.tensor_scalar(
                slot6[:, :], i6_all[:, 6 * g:6 * g + 6], float(CAP), None,
                op0=ALU.mult), waits=[("ve", None)])
            ve_op(lambda e: e.tensor_tensor(slot6[:, :], slot6[:, :],
                                            pos6[:, :], op=ALU.add),
                  waits=[("ve", None)])
            ve_op(lambda e: e.tensor_scalar(vm6[:, :], pos6[:, :],
                                            float(CAP), None, op0=ALU.is_lt),
                  waits=[("ve", None)])
            ve_op(lambda e: e.tensor_scalar(nv6[:, :], vm6[:, :], -1.0, 1.0,
                                            op0=ALU.mult, op1=ALU.add),
                  waits=[("ve", None)])
            ve_op(lambda e: e.tensor_tensor(tt1[:, :], vm6[:, :],
                                            slot6[:, :], op=ALU.mult),
                  waits=[("ve", None)])
            ve_op(lambda e: e.tensor_tensor(tt2[:, :], nv6[:, :],
                                            inval6[:, :], op=ALU.mult),
                  waits=[("ve", None)])
            ve_op(lambda e: e.tensor_tensor(sl_f[:, :], tt1[:, :],
                                            tt2[:, :], op=ALU.add),
                  waits=[("ve", None)])
            ve_op(lambda e: e.tensor_scalar(tt2[:, :], nv6[:, :],
                                            float(NSLOT), None,
                                            op0=ALU.mult),
                  waits=[("ve", None)])
            ve_op(lambda e: e.tensor_tensor(gs_f[:, :], tt1[:, :],
                                            tt2[:, :], op=ALU.add),
                  waits=[("ve", None)])
            ve_op(lambda e, g=g: e.tensor_copy(
                out=sl_all[:, 6 * g:6 * g + 6], in_=sl_f[:, :]),
                waits=[("ve", None)])
            vew_sl = now("ve")
            for k in range(TOPK):
                def xs_scat(e, g=g, k=k, gg=(g % 2)):
                    if "breg" not in hold:
                        hold["breg"] = e.to_reg(NSLOT - 1)
                    return e.indirect_dma_start(
                        out=xs_buf,
                        out_offset=IndirectOffsetOnAxis(
                            ap=sl_all[:, 6 * g + k:6 * g + k + 1], axis=0),
                        in_=src_x[:, H * gg:H * (gg + 1)], in_offset=None,
                        bounds_check=hold["breg"], oob_is_err=False)
                sch.op("gpsimd", xs_scat, incs=[("pq", 16)],
                       waits=([("ve", vew_sl), ("lx", hold[("sxl", g)])]
                              if k == 0 else ()))
            hold[("scat", g)] = now("pq")
            ve_op(lambda e: e.tensor_copy(out=gs_u[:, 0:6], in_=gs_f[:, :]),
                  waits=[("ve", None), ("lg", None)])
            dma("sync", "lg", gs_sp[128 * g:128 * (g + 1), :], gs_u[:, :],
                waits=[("ve", None)])
        sch.barrier()

        if _PHASE <= 5:
            _truncate()
            return nc

        s_p3.close()
        open_scopes.remove(s_p3)
        s_p5 = ExitStack()
        open_scopes.append(s_p5)
        wex1 = s_p5.enter_context(
            nc.sbuf_tensor("wex1", [128, 8 * FE], BF16))
        wex2 = s_p5.enter_context(nc.sbuf_tensor("wex2", [128, 4 * H], BF16))
        wex1b = s_p5.enter_context(
            nc.sbuf_tensor("wex1b", [128, 8 * FE], BF16))
        wex2b = s_p5.enter_context(
            nc.sbuf_tensor("wex2b", [128, 4 * H], BF16))
        xs_sb = s_p5.enter_context(
            nc.sbuf_tensor("xs_sb", [128, 3 * 1024], BF16))
        xst = s_p5.enter_context(nc.sbuf_tensor("xst", [128, 8 * CAP], BF16))
        h1g = s_p5.enter_context(nc.sbuf_tensor("h1g", [128, 4 * CAP], BF16))
        y_sb = s_p5.enter_context(nc.sbuf_tensor("y_sb", [128, H], BF16))
        w2stg = s_p5.enter_context(
            nc.sbuf_tensor("w2stg", [128, 4 * H], F32))
        w2stgb = s_p5.enter_context(
            nc.sbuf_tensor("w2stgb", [128, 4 * H], F32))
        sch.op("sync", lambda e: e.nop(), waits=[("pq", None)])
        buf_last_use = {0: 0, 1: 0}
        wtarget = {}
        caststg = {0: 0, 1: 0}

        def issue_w(e_w, pew):
            wb1 = wex1 if e_w % 2 == 0 else wex1b
            wb2 = wex2 if e_w % 2 == 0 else wex2b
            stg = w2stg if e_w % 2 == 0 else w2stgb
            wsem = "w0" if e_w % 2 == 0 else "w1"
            for c in range(8):
                sch.op("gpsimd", lambda e, wb1=wb1, e_w=e_w, c=c:
                       e.dma_start(
                           out=wb1[:, 512 * c:512 * (c + 1)],
                           in_=w1_in[H * e_w + 128 * c:
                                     H * e_w + 128 * (c + 1), :]),
                       incs=[(wsem, 16)],
                       waits=[("pe", pew)] if c == 0 else ())
            for c in range(2):
                sch.op("sync", lambda e, stg=stg, e_w=e_w, c=c:
                       e.dma_start(
                           out=stg[:, 1024 * c:1024 * (c + 1)],
                           in_=w2_in[FE * e_w + 128 * c:
                                     FE * e_w + 128 * (c + 1), :]),
                       incs=[(wsem, 16)],
                       waits=[("ac", caststg[e_w % 2])] if c == 0 else ())
            for c in range(2, 4):
                sch.op("scalar", lambda e, stg=stg, e_w=e_w, c=c:
                       e.dma_start(
                           out=stg[:, 1024 * c:1024 * (c + 1)],
                           in_=w2_in[FE * e_w + 128 * c:
                                     FE * e_w + 128 * (c + 1), :]),
                       incs=[(wsem, 16)],
                       waits=[("ac", caststg[e_w % 2])] if c == 2 else ())
            wtarget[e_w] = sch.cnt[wsem]
            for c in range(4):
                ac_op(lambda e, wb2=wb2, stg=stg, c=c: e.activation(
                    out=wb2[:, 1024 * c:1024 * (c + 1)],
                    in_=stg[:, 1024 * c:1024 * (c + 1)], func=AF.Copy),
                    waits=[("pe", pew), (wsem, wtarget[e_w])] if c == 0
                    else ())
            caststg[e_w % 2] = sch.cnt["ac"]

        issue_w(0, 0)
        issue_w(1, 0)
        for e_i in range(E):
            wb1 = wex1 if e_i % 2 == 0 else wex1b
            wb2 = wex2 if e_i % 2 == 0 else wex2b
            wsem = "w0" if e_i % 2 == 0 else "w1"
            pqw = wtarget[e_i]
            for r in range(3):
                dma("sync", "ld", xs_sb[:, 1024 * r:1024 * (r + 1)],
                    xs_buf[CAP * e_i + 128 * r:CAP * e_i + 128 * (r + 1), :],
                    waits=[("pe", hold.get("xs_used", 0))] if r == 0 else ())
            ldw = now("ld")
            if _ESTG <= 2:
                continue
            psdb = ps_d[:, 0:256].bitcast(BF16)
            tcnt = 0
            for hc in range(8):
                for r in range(3):
                    dstv = (ps_t[:, 0:128] if tcnt % 2 == 0
                            else psdb[:, 0:128])
                    pe_op(lambda e, hc=hc, r=r, dstv=dstv: e.transpose(
                        dstv,
                        xs_sb[:, 1024 * r + 128 * hc:1024 * r + 128 * (hc + 1)],
                        idt_b[:, :]),
                        waits=[("ld", ldw), ("ve", None)] if tcnt == 0
                        else ([("ve", hold.get(("xev", e_i, tcnt - 2), 0))]
                              if tcnt >= 2 else ()))
                    ve_op(lambda e, hc=hc, r=r, dstv=dstv: e.tensor_copy(
                        out=xst[:, CAP * hc + 128 * r:CAP * hc + 128 * (r + 1)],
                        in_=dstv),
                        waits=[("pe", None)])
                    hold[("xev", e_i, tcnt)] = now("ve")
                    tcnt += 1
            hold["xs_used"] = now("pe")
            if _ESTG <= 3:
                continue
            acg = now("ac")
            for fc in range(4):
                for hc in range(8):
                    w = []
                    if hc == 0:
                        w = [("ve", None), (wsem, pqw)] if fc == 0 \
                            else [("ac", acg)]
                    pe_op(lambda e, fc=fc, hc=hc, wb1=wb1: e.matmul(
                        ps_b[:, 0:CAP],
                        wb1[:, 512 * hc + 128 * fc:512 * hc + 128 * (fc + 1)],
                        xst[:, CAP * hc:CAP * (hc + 1)],
                        start=(hc == 0), stop=(hc == 7)), waits=w)
                ac_op(lambda e, fc=fc: e.activation(
                    out=h1g[:, CAP * fc:CAP * (fc + 1)], in_=ps_b[:, 0:CAP],
                    func=AF.Gelu_apprx_tanh), waits=[("pe", None)])
                acg = now("ac")
            if _ESTG <= 4:
                continue
            for r in range(3):
                for nh in range(2):
                    psX = ps_c if nh == 0 else ps_d
                    pskey = "yev_c" if nh == 0 else "yev_d"
                    for fc in range(4):
                        w = []
                        if fc == 0 and nh == 0 and r == 0:
                            w = [("ac", None)]
                        if fc == 0:
                            w = w + [("ve", hold.get(pskey, 0))]
                        pe_op(lambda e, r=r, nh=nh, fc=fc, psX=psX, wb2=wb2:
                              e.matmul(
                                  psX[:, :],
                                  h1g[:, CAP * fc + 128 * r:
                                       CAP * fc + 128 * (r + 1)],
                                  wb2[:, 1024 * fc + 512 * nh:
                                       1024 * fc + 512 * (nh + 1)],
                                  start=(fc == 0), stop=(fc == 3)), waits=w)
                    ve_op(lambda e, r=r, nh=nh, psX=psX: e.tensor_copy(
                        out=y_sb[:, 512 * nh:512 * (nh + 1)], in_=psX[:, :]),
                        waits=[("pe", None), ("ly", None)])
                    hold[pskey] = now("ve")
                dma("scalar", "ly",
                    y_buf[CAP * e_i + 128 * r:CAP * e_i + 128 * (r + 1), :],
                    y_sb[:, :], waits=[("ve", None)])
            buf_last_use[e_i % 2] = now("pe")
            if e_i + 2 < E:
                issue_w(e_i + 2, buf_last_use[e_i % 2])
        dma("gpsimd", "pq", sw1_bfd, sw1_in)
        dma("gpsimd", "pq", sw2_bfd, sw2_in)
        sch.barrier()

        if _PHASE <= 7:
            _truncate()
            return nc

        s_p5.close()
        open_scopes.remove(s_p5)
        s_p6 = ExitStack()
        open_scopes.append(s_p6)
        sw1t = s_p6.enter_context(
            nc.sbuf_tensor("sw1t", [128, 8 * SF], BF16))
        sw2t = s_p6.enter_context(
            nc.sbuf_tensor("sw2t", [128, 16 * H], BF16))
        shp = s_p6.enter_context(nc.sbuf_tensor("shp", [128, 8 * 512], BF16))
        h1gs = s_p6.enter_context(
            nc.sbuf_tensor("h1gs", [128, 16 * 512], BF16))
        hi_own = s_p6.enter_context(
            nc.sbuf_tensor("hi_own", [128, 4 * H], BF16))
        ygth = s_p6.enter_context(
            nc.sbuf_tensor("ygth", [128, 6 * H], BF16))
        pwg = s_p6.enter_context(nc.sbuf_tensor("pwg", [128, 8], BF16))
        pwg_f = s_p6.enter_context(nc.sbuf_tensor("pwg_f", [128, 6], F32))
        ytmp = s_p6.enter_context(nc.sbuf_tensor("ytmp", [128, 512], F32))
        xm_t = s_p6.enter_context(nc.sbuf_tensor("xm_t", [128, H], F32))
        out_t = s_p6.enter_context(nc.sbuf_tensor("out_t", [128, H], F32))
        for c in range(8):
            dmat(sw1t[:, SF * c:SF * (c + 1)],
                 sw1_bfd[0:SF, 128 * c:128 * (c + 1)],
                 waits=[("pq", None), ("ly", None), ("ld", None),
                        ("pe", None), ("ve", None)] if c == 0 else ())
        for c in range(16):
            dmat(sw2t[:, H * c:H * (c + 1)],
                 sw2_bfd[0:H, 128 * c:128 * (c + 1)])
        sch.op("gpsimd", lambda e: e.nop(),
               waits=[("ld", None), ("pq", None), ("ly", None),
                      ("lo", None), ("lg", None), ("s0", None)])
        sch.op("tensor", lambda e: e.nop(), waits=[("sq", None)])
        for c4 in range(4):
            for j in range(4):
                sch.op("gpsimd", lambda e, c4=c4, j=j: e.indirect_dma_start(
                    out=hi_own[:, H * j:H * (j + 1)], out_offset=None,
                    in_=xln2_bf,
                    in_offset=IndirectOffsetOnAxis(
                        ap=oidx[:, 4 * c4 + j:4 * c4 + j + 1], axis=0)),
                    incs=[("pq", 16)],
                    waits=[("pe", hold.get("shp_used", 0))] if j == 0 else ())
            pqw = now("pq")
            po_op(lambda e: e.nop(), waits=[("pq", pqw)])
            pow_ = now("po")
            psdb6 = ps_d[:, 0:256].bitcast(BF16)
            tcnt = 0
            for j in range(4):
                for hc in range(8):
                    dstv = (ps_t[:, 0:128] if tcnt % 2 == 0
                            else psdb6[:, 0:128])
                    w = [("po", pow_), ("ve", None)] if tcnt == 0 else ()
                    if tcnt >= 2:
                        w = [("ve", hold.get(("sev", c4, tcnt - 2), 0))]
                    pe_op(lambda e, j=j, hc=hc, dstv=dstv: e.transpose(
                        dstv,
                        hi_own[:, H * j + 128 * hc:H * j + 128 * (hc + 1)],
                        idt_b[:, :]), waits=w)
                    ve_op(lambda e, j=j, hc=hc, dstv=dstv: e.tensor_copy(
                        out=shp[:, 512 * hc + 128 * j:
                                512 * hc + 128 * (j + 1)],
                        in_=dstv),
                        waits=[("pe", None)])
                    hold[("sev", c4, tcnt)] = now("ve")
                    tcnt += 1
            acg0 = now("ac")
            for fc in range(16):
                for hc in range(8):
                    w = []
                    if hc == 0:
                        w = [("ve", None)] if fc == 0 else [("ac", acg0)]
                    pe_op(lambda e, fc=fc, hc=hc: e.matmul(
                        ps_b[:, :],
                        sw1t[:, SF * hc + 128 * fc:SF * hc + 128 * (fc + 1)],
                        shp[:, 512 * hc:512 * (hc + 1)],
                        start=(hc == 0), stop=(hc == 7)), waits=w)
                ac_op(lambda e, fc=fc: e.activation(
                    out=h1gs[:, 512 * fc:512 * (fc + 1)], in_=ps_b[:, :],
                    func=AF.Gelu_apprx_tanh), waits=[("pe", None)])
                acg0 = now("ac")
            hold["shp_used"] = now("pe")
            for j in range(4):
                gown = 4 * c4 + j
                sch.op("gpsimd", lambda e, c4=c4, j=j: e.indirect_dma_start(
                    out=gsl[:, :], out_offset=None,
                    in_=gs_sp,
                    in_offset=IndirectOffsetOnAxis(
                        ap=oidx[:, 4 * c4 + j:4 * c4 + j + 1], axis=0)),
                    incs=[("pq", 16)],
                    waits=[("ve", hold.get("out_done", 0))])
                pq_gsl = now("pq")
                for k in range(TOPK):
                    sch.op("gpsimd", lambda e, k=k:
                           e.indirect_dma_start(
                               out=ygth[:, H * k:H * (k + 1)],
                               out_offset=None,
                               in_=y_buf,
                               in_offset=IndirectOffsetOnAxis(
                                   ap=gsl[:, k:k + 1], axis=0)),
                           incs=[("pq", 16)],
                           waits=[("pq", pq_gsl)] if k == 0 else ())
                sch.op("gpsimd", lambda e, c4=c4, j=j: e.indirect_dma_start(
                    out=pwg[:, 0:6], out_offset=None,
                    in_=pw_sp,
                    in_offset=IndirectOffsetOnAxis(
                        ap=oidx[:, 4 * c4 + j:4 * c4 + j + 1], axis=0)),
                    incs=[("pq", 16)])
                sch.op("gpsimd", lambda e, c4=c4, j=j: e.indirect_dma_start(
                    out=xm_t[:, :], out_offset=None,
                    in_=xmid_sp,
                    in_offset=IndirectOffsetOnAxis(
                        ap=oidx[:, 4 * c4 + j:4 * c4 + j + 1], axis=0)),
                    incs=[("pq", 16)])
                po_op(lambda e: e.nop(), waits=[("pq", now("pq"))])
                pot = now("po")
                ve_op(lambda e: e.tensor_copy(out=pwg_f[:, :],
                                              in_=pwg[:, 0:6]),
                      waits=[("po", pot)])
                for nh in range(2):
                    psX = ps_c if nh == 0 else ps_d
                    pskey = "oev_c" if nh == 0 else "oev_d"
                    for fc in range(16):
                        w = []
                        if fc == 0:
                            w = [("ac", None),
                                 ("ve", hold.get(pskey, 0))]
                        pe_op(lambda e, j=j, nh=nh, fc=fc, psX=psX: e.matmul(
                            psX[:, :],
                            h1gs[:, 512 * fc + 128 * j:
                                 512 * fc + 128 * (j + 1)],
                            sw2t[:, 1024 * fc + 512 * nh:
                                 1024 * fc + 512 * (nh + 1)],
                            start=(fc == 0), stop=(fc == 15)), waits=w)
                    ve_op(lambda e, nh=nh, psX=psX: e.tensor_tensor(
                        out_t[:, 512 * nh:512 * (nh + 1)], psX[:, :],
                        xm_t[:, 512 * nh:512 * (nh + 1)], op=ALU.add),
                        waits=[("pe", None), ("po", pot), ("ld", None)])
                    for k in range(TOPK):
                        ve_op(lambda e, nh=nh, k=k: e.tensor_scalar(
                            ytmp[:, :],
                            ygth[:, H * k + 512 * nh:H * k + 512 * (nh + 1)],
                            pwg_f[:, k:k + 1], None, op0=ALU.mult),
                            waits=[("ve", None)])
                        ve_op(lambda e, nh=nh: e.tensor_tensor(
                            out_t[:, 512 * nh:512 * (nh + 1)],
                            out_t[:, 512 * nh:512 * (nh + 1)],
                            ytmp[:, :], op=ALU.add),
                            waits=[("ve", None)])
                    hold[pskey] = now("ve")
                dma("sync", "ld", out_ext[128 * gown:128 * (gown + 1), :],
                    out_t[:, :], waits=[("ve", None)])
                hold["out_done"] = now("ve")
        sch.op("sync", lambda e: e.nop(),
               waits=[("ld", None), ("pq", None), ("sq", None),
                      ("lx", None), ("lo", None), ("lg", None),
                      ("ly", None), ("s0", None), ("s1", None),
                      ("w0", None), ("w1", None)])
        s_p6.close()
        open_scopes.remove(s_p6)
        block = ctx.enter_context(nc.Block())
        sch.emit(block)
    return nc


_NC_CACHE = {}


def _build_cached():
    if "nc" not in _NC_CACHE:
        _NC_CACHE["nc"] = build()
    return _NC_CACHE["nc"]


def kernel(L0_attn_x, L0_attn_Wq, L0_attn_Wk, L0_attn_Wv,
           L0_ffn_router_weight, L0_ffn_expert_bias, L0_ffn_w1, L0_ffn_w2,
           L0_ffn_sw1, L0_ffn_sw2):
    x2d = np.ascontiguousarray(np.asarray(L0_attn_x, np.float32).reshape(T, H))
    wq = np.ascontiguousarray(np.asarray(L0_attn_Wq, np.float32))
    wk = np.ascontiguousarray(np.asarray(L0_attn_Wk, np.float32))
    wv = np.ascontiguousarray(np.asarray(L0_attn_Wv, np.float32))
    rw = np.ascontiguousarray(np.asarray(L0_ffn_router_weight, np.float32))
    eb = np.ascontiguousarray(np.asarray(L0_ffn_expert_bias,
                                         np.float32).reshape(1, E))
    w1 = np.ascontiguousarray(np.asarray(L0_ffn_w1,
                                         np.float32).reshape(E * H, FE))
    w2 = np.ascontiguousarray(np.asarray(L0_ffn_w2,
                                         np.float32).reshape(E * FE, H))
    sw1 = np.ascontiguousarray(np.asarray(L0_ffn_sw1, np.float32))
    sw2 = np.ascontiguousarray(np.asarray(L0_ffn_sw2, np.float32))

    nc = _build_cached()
    in_maps = []
    for c in range(NCORE):
        tiles = np.arange(c * NTPC, (c + 1) * NTPC, dtype=np.uint32)
        oidx_np = (tiles[None, :] * 128
                   + np.arange(128, dtype=np.uint32)[:, None])
        in_maps.append({
            "x_in": x2d, "wq_in": wq, "wk_in": wk, "wv_in": wv,
            "rw_in": rw, "bias_in": eb, "w1_in": w1, "w2_in": w2,
            "sw1_in": sw1, "sw2_in": sw2,
            "oidx_in": np.ascontiguousarray(oidx_np),
        })
    res = run_bass_kernel_spmd(nc, in_maps, core_ids=list(range(NCORE)))
    out = np.concatenate([res.results[c]["out_ext"] for c in range(NCORE)],
                         axis=0)
    return out.reshape(B, SEQ, H)
